# version 36
# baseline (speedup 1.0000x reference)
import contextlib
import numpy as np
import concourse.bass as bass
import concourse.mybir as mybir
from concourse.bass_utils import run_bass_kernel_spmd
from concourse.bass_core_rust import IndirectOffsetOnAxis

F32 = mybir.dt.float32
BF16 = mybir.dt.bfloat16
U32 = mybir.dt.uint32
U8 = mybir.dt.uint8
AF = mybir.ActivationFunctionType
ALU = mybir.AluOpType
AX = mybir.AxisListType

CH_INFO = {
    "pe":    ("tensor", 1),
    "act":   ("scalar", 1),
    "dve":   ("vector", 1),
    "pool":  ("gpsimd", 1),
    "sp":    ("sync", 1),
    "qsp":   ("sync", 16),
    "qact":  ("scalar", 16),
    "qpool": ("gpsimd", 16),
}
COMPUTE_CH = ("pe", "act", "dve", "pool", "sp")
ENGINE_OF = {c: CH_INFO[c][0] for c in CH_INFO}


class Op:
    __slots__ = ("ch", "eng", "emit", "reads", "writes", "gidx", "deps",
                 "need_inc", "cum", "waits", "guar", "tag", "sem", "reuse_wait")

    def __init__(self, ch, emit, reads, writes, gidx, tag):
        self.ch = ch
        self.eng = ENGINE_OF[ch]
        self.emit = emit
        self.reads = reads
        self.writes = writes
        self.gidx = gidx
        self.deps = []
        self.need_inc = False
        self.cum = None
        self.waits = None
        self.guar = None
        self.tag = tag


class Prog:
    def __init__(self):
        self.ops = []
        self.last_writer = {}
        self.readers = {}
        self.by_engine = {}

    def op(self, ch, emit, reads=(), writes=(), tag="", relax_raw=False):
        o = Op(ch, emit, tuple(reads), tuple(writes), len(self.ops), tag)
        deps = {}
        for r in o.reads:
            w = self.last_writer.get(r)
            if w is not None:
                deps[w.gidx] = (w, "raw")
        for w_ in o.writes:
            lw = self.last_writer.get(w_)
            if lw is not None and lw.gidx not in deps:
                deps[lw.gidx] = (lw, "waw")
            for rd in self.readers.get(w_, ()):
                if rd.gidx not in deps:
                    deps[rd.gidx] = (rd, "war")
        for r in o.reads:
            self.readers.setdefault(r, []).append(o)
        for w_ in o.writes:
            self.last_writer[w_] = o
            self.readers[w_] = []
        kept = []
        for d, kind in deps.values():
            if d is o:
                continue
            if d.ch == o.ch and d.ch in COMPUTE_CH:
                if kind == "waw" or (kind == "raw" and relax_raw):
                    continue
            elif d.ch == o.ch:
                pass
            kept.append(d)
        o.deps = kept
        self.ops.append(o)
        self.by_engine.setdefault(o.eng, []).append(o)
        return o

    SEM_POOL = {"qsp": 6, "qact": 4, "qpool": 12}

    def finalize(self):
        for o in self.ops:
            best = {}
            extra = []
            for d in o.deps:
                if d.ch in COMPUTE_CH:
                    if d.ch not in best or d.gidx > best[d.ch].gidx:
                        best[d.ch] = d
                else:
                    extra.append(d)
            o.deps = list(best.values()) + extra
        for o in self.ops:
            if o.ch not in COMPUTE_CH:
                o.need_inc = True
            for d in o.deps:
                d.need_inc = True
        cum = {}
        pool_state = {}
        for o in self.ops:
            if o.ch in COMPUTE_CH:
                o.sem = o.ch
                if o.need_inc:
                    cum[o.sem] = cum.get(o.sem, 0) + 1
                o.cum = cum.get(o.sem, 0)
                o.reuse_wait = None
            else:
                st = pool_state.setdefault(o.ch, [0, {}])
                sid = st[0] % self.SEM_POOL[o.ch]
                st[0] += 1
                o.sem = f"{o.ch}_{sid}"
                prev = st[1].get(sid)
                o.reuse_wait = (o.sem, prev.cum, prev) if prev is not None else None
                cum[o.sem] = cum.get(o.sem, 0) + 16
                o.cum = cum[o.sem]
                st[1][sid] = o
        prev_on_channel = {}
        for o in self.ops:
            g = dict(prev_on_channel[o.ch].guar) if o.ch in prev_on_channel and o.ch in COMPUTE_CH else {}
            if o.ch not in COMPUTE_CH:
                pass
            g[o.sem] = max(g.get(o.sem, 0), o.cum)
            o.waits = []
            if o.reuse_wait is not None:
                s_, v_, d_ = o.reuse_wait
                o.waits.append((s_, v_, d_))
            for d in o.deps:
                o.waits.append((d.sem, d.cum, d))
            for s_, v_, d_ in o.waits:
                if d_ is not None:
                    for c, v in d_.guar.items():
                        if v > g.get(c, 0):
                            g[c] = v
            o.guar = g
            prev_on_channel[o.ch] = o
        known = {}
        for o in self.ops:
            k = known.setdefault(o.eng, {})
            keep = []
            for s_, v, d in sorted(o.waits, key=lambda t: -t[1]):
                if k.get(s_, 0) >= v:
                    continue
                keep.append((s_, v))
                k[s_] = v
                if d is not None:
                    for cc, vv in d.guar.items():
                        if vv > k.get(cc, 0):
                            k[cc] = vv
            o.waits = keep
        self.final_cum = cum
        return cum

    def final_dma_waits(self, engine, ch):
        return [(engine, s, v) for s, v in self.final_cum.items()
                if s.startswith(ch + "_")]

    def emit(self, nc, final_waits=None, verbose=False):
        cum = self.finalize()
        if verbose:
            n_w = sum(len(o.waits) for o in self.ops)
            n_i = sum(1 for o in self.ops if o.need_inc)
            tot = {c: sum(v for s, v in cum.items() if s == c or s.startswith(c + "_"))
                   for c in CH_INFO}
            print(f"fw: {len(self.ops)} ops, {n_i} incs, {n_w} waits, cum={tot}")
        with contextlib.ExitStack() as stack:
            sems = {}
            names = set(cum) | {fw[1] for fw in (final_waits or [])}
            for s_ in sorted(names):
                sems[s_] = stack.enter_context(nc.semaphore(name=f"sem_{s_}"))
            with nc.Block() as block:
                for eng_name in ("sync", "tensor", "scalar", "vector", "gpsimd"):
                    ops = self.by_engine.get(eng_name, [])
                    fwaits = [fw for fw in (final_waits or []) if fw[0] == eng_name]
                    if not ops and not fwaits:
                        continue

                    def body(eng, ops=ops, fwaits=fwaits):
                        for o in ops:
                            for s_, v in o.waits:
                                eng.wait_ge(sems[s_], v)
                            inst = o.emit(eng)
                            if o.need_inc:
                                inst.then_inc(sems[o.sem],
                                              16 if o.ch not in COMPUTE_CH else 1)
                        for _, s_, v in fwaits:
                            eng.wait_ge(sems[s_], v)

                    getattr(block, eng_name)(body)
        return cum


B, S, C = 4, 4096, 1024
NH, HD = 16, 64
FF, NE = 2048, 8
TOK = 2048
NT_OWN, NT_ALL = 16, 32
NDC = 8
NFC = 16
EPS = 1e-5
QSCALE = 1.0 / 32.0
GROUP_HEADS = 4
NGROUP = NH // GROUP_HEADS
GC = GROUP_HEADS * HD
VW = GROUP_HEADS * (HD + 1)

SBUF_END = 229344

_L = {}
_off = [18560]


def _place(name, nbytes):
    _L[name] = _off[0]
    _off[0] += (nbytes + 63) & ~63
    return _L[name]


for _n, _b in [("ident", 512), ("ones_bf", 256), ("gw_sb", 256), ("ones8", 32),
               ("scores", 64), ("rmax_all", 64), ("alpha", 64), ("beta", 64),
               ("logits_all", 512), ("mask_all", 512)]:
    _place(_n, _b)
for _pre, _ncols in [("L1", 32), ("L2", 16), ("L3", 16), ("L4", 16)]:
    for _nm in ("s", "ss", "m", "mq", "v", "lv", "a", "b"):
        _place(f"{_pre}_{_nm}", 4 * _ncols)
for _n, _b in [("eqb", 32), ("prefb", 32), ("epsc", 64), ("scr", 4096),
               ("x1b0", 4096), ("x1b1", 4096), ("ln3b", 4096),
               ("bo_bf", 2048), ("b_bcast", 4096), ("tmpf0", 512), ("tmpf1", 512)]:
    _place(_n, _b)
_PEND = (_off[0] + 63) & ~63

_off[0] = _PEND
for _dc in range(NDC):
    _place(f"O_fm{_dc}", 4096)
for _dc in range(NDC):
    _place(f"ln1_fm{_dc}", 8192)
for _n, _b in [("xt0", 4096), ("xt1", 4096), ("wq_sb", 4096), ("wk_sb", 4096),
               ("wv_sb", 4096), ("Q_sb0", 4096), ("Q_sb1", 4096),
               ("K_sb0", 8192), ("K_sb1", 8192), ("V_sb", 16640),
               ("A01_0", 2048), ("A01_1", 2048), ("A23_0", 2048), ("A23_1", 2048),
               ("srow", 2048), ("trow", 2048), ("rse_bf", 1024),
               ("rse_t0", 2048), ("rse_t1", 2048)]:
    _place(_n, _b)
assert _off[0] <= SBUF_END, f"region A overflow: {_off[0]}"
_L["p0q0"] = _L["Q_sb0"]
_L["p0q1"] = _L["Q_sb1"]
_L["p0k0a"] = _L["K_sb0"]
_L["p0k0b"] = _L["K_sb0"] + 4096

_off[0] = _L["ln1_fm0"]
for _dc in range(NDC):
    _place(f"fm{_dc}", 4096)
for _n, _b in [("wo_sb", 16384), ("xo0", 4096), ("xo1", 4096),
               ("tb1_0", 4096), ("tb1_1", 4096), ("aob0", 4096), ("aob1", 4096),
               ("zb0", 4096), ("zb1", 4096), ("ln3c0", 4096), ("ln3c1", 4096)]:
    _place(_n, _b)
assert _off[0] <= SBUF_END

_off[0] = _L["O_fm0"]
for _n, _b in [("wd_b0", 8192), ("wd_b1", 8192), ("wgu_b0", 4096), ("wgu_b1", 4096),
               ("silb0", 2048), ("silb1", 2048)]:
    _place(_n, _b)
assert _off[0] <= _L["ln1_fm0"], "region C head overflows into fm"
_off[0] = _L["wo_sb"]
_place("ffn_acc", 65536)
_place("pe_half", 32768)
for _dc in range(NDC):
    _place(f"xef{_dc}", 1024)
for _n, _b in [("ident_bf", 256), ("gbuf0", 2048), ("gbuf1", 2048), ("gbuf2", 2048),
               ("Ltri", 256), ("mask_em", 512), ("c_all", 512), ("ones16", 64),
               ("T8b", 64), ("Pexcl", 64), ("slotg", 64), ("slotu", 64),
               ("selm", 64), ("selmu", 64), ("t1b", 64), ("tok_iota", 64),
               ("pref_sb", 128), ("idx_sb", 128)]:
    _place(_n, _b)
assert _off[0] <= SBUF_END, f"region C overflow: {_off[0]}"
CAP = 384
CAPK = CAP // 128
NSLOT = NE * CAPK


class Builder:
    def __init__(self, nc):
        self.nc = nc
        self.p = Prog()
        self.t = {}

        d = nc.dram_tensor
        self.x_in = d("x_in", [S, C], F32, kind="ExternalInput")
        self.wqT = d("wqT", [C, 3 * C], F32, kind="ExternalInput")
        self.woT = d("woT", [C, C], F32, kind="ExternalInput")
        self.bo = d("bo", [1, C], F32, kind="ExternalInput")
        self.gwT = d("gwT", [C, NE], F32, kind="ExternalInput")
        self.wsgT = d("wsgT", [C, FF], F32, kind="ExternalInput")
        self.wsuT = d("wsuT", [C, FF], F32, kind="ExternalInput")
        self.wsdT = d("wsdT", [FF, C], F32, kind="ExternalInput")
        self.wrgT = d("wrgT", [NE, C, FF], F32, kind="ExternalInput")
        self.wruT = d("wruT", [NE, C, FF], F32, kind="ExternalInput")
        self.wrdT = d("wrdT", [NE, FF, C], F32, kind="ExternalInput")
        self.y = d("y", [TOK, C], F32, kind="ExternalOutput")
        self.x1sp = d("x1sp", [TOK, C], F32, kind="Internal")
        self.xssp = d("xssp", [TOK, C], BF16, kind="Internal")
        self.rout_sp = d("rout_sp", [TOK + NE * CAP, C], F32, kind="Internal")
        self.wsgu_bf = d("wsgu_bf", [C, NFC, 256], BF16, kind="Internal")
        self.wsd_bf = d("wsd_bf", [FF, C], BF16, kind="Internal")
        self.wrgu_bf = d("wrgu_bf", [NE, C, NFC, 256], BF16, kind="Internal")
        self.wrd_bf = d("wrd_bf", [NE, FF, C], BF16, kind="Internal")
        self.idx_dram = d("idx_dram", [NE * CAP, 1], mybir.dt.uint32, kind="Internal")

    def at(self, name, shape, dt=F32):
        h = self.nc.alloc_sbuf_tensor_at(name, shape, dt, offset=_L[name])
        self.t[name] = h
        return h

    def psum_setup(self):
        self._psg = self.nc.psum_tensor("ps_big", [128, 2048], F32)
        self._psa = self.nc.psum_tensor("ps_av", [128, 2048], F32)
        self.ps_big = self._psg.__enter__()
        self.ps_av = self._psa.__enter__()

    def pb(self, c0, c1, p0=0, p1=128):
        keys = tuple(f"pb{b}" for b in range(c0 // 512, (c1 + 511) // 512))
        return self.ps_big[p0:p1, c0:c1], keys

    def pa(self, c0, c1, p0=0, p1=128):
        keys = tuple(f"pa{b}" for b in range(c0 // 512, (c1 + 511) // 512))
        return self.ps_av[p0:p1, c0:c1], keys

    def ln_stats(self, x_ap, x_keys, pre, col, scr, scr_key):
        p, t = self.p, self.t
        s_ = t[pre + "_s"][:, col:col + 1]
        ss = t[pre + "_ss"][:, col:col + 1]
        m_ = t[pre + "_m"][:, col:col + 1]
        mq = t[pre + "_mq"][:, col:col + 1]
        v_ = t[pre + "_v"][:, col:col + 1]
        lv = t[pre + "_lv"][:, col:col + 1]
        a_ = t[pre + "_a"][:, col:col + 1]
        b_ = t[pre + "_b"][:, col:col + 1]
        ks = lambda n: f"{pre}_{n}{col}"
        p.op("dve", lambda e: e.reduce_sum(s_, x_ap, axis=AX.X),
             reads=x_keys, writes=[ks("s")], tag=f"{pre}{col}_rs")
        p.op("act", lambda e: e.activation(scr, x_ap, AF.Square, accum_out=ss),
             reads=x_keys, writes=[ks("ss"), scr_key], tag=f"{pre}{col}_sq")
        p.op("dve", lambda e: e.tensor_single_scalar(m_, s_, 1.0 / C, ALU.mult),
             reads=[ks("s")], writes=[ks("m")])
        p.op("dve", lambda e: e.tensor_mul(mq, m_, m_),
             reads=[ks("m")], writes=[ks("mq")])
        p.op("dve", lambda e: e.tensor_scalar(v_, ss, 1.0 / C, mq, ALU.mult, ALU.subtract),
             reads=[ks("ss"), ks("mq")], writes=[ks("v")])
        p.op("act", lambda e: e.activation(lv, v_, AF.Ln, bias=self.epsc[:]),
             reads=[ks("v"), "epsc"], writes=[ks("lv")])
        p.op("act", lambda e: e.activation(a_, lv, AF.Exp, scale=-0.5),
             reads=[ks("lv")], writes=[ks("a")])
        p.op("dve", lambda e: e.tensor_scalar(b_, m_, a_, -1.0, ALU.mult, ALU.mult),
             reads=[ks("m"), ks("a")], writes=[ks("b")])
        return a_, b_, (ks("a"), ks("b"))

    def ln_apply(self, out_ap, out_keys, x_ap, x_keys, a_, b_, ab_keys, tag=""):
        self.p.op("dve",
                  lambda e: e.tensor_scalar(out_ap, x_ap, a_, b_, ALU.mult, ALU.add),
                  reads=tuple(x_keys) + tuple(ab_keys), writes=out_keys, tag=tag)

    def build(self):
        nc, p = self.nc, self.p
        t = self.t
        self.psum_setup()

        ident = self.at("ident", [128, 128], F32)
        ones_bf = self.at("ones_bf", [1, 128], BF16)
        gw_sb = self.at("gw_sb", [128, NDC * NE], F32)
        ones8 = self.at("ones8", [128, NE], F32)
        scores = self.at("scores", [128, NT_OWN], F32)
        rmax_all = self.at("rmax_all", [128, NT_OWN], F32)
        alpha = self.at("alpha", [128, NT_OWN], F32)
        beta = self.at("beta", [128, NT_OWN], F32)
        logits_all = self.at("logits_all", [128, NT_OWN * NE], F32)
        mask_all = self.at("mask_all", [128, NT_OWN * NE], F32)
        for pre, ncols in [("L1", NT_ALL), ("L2", NT_OWN), ("L3", NT_OWN), ("L4", NT_OWN)]:
            for nm in ("s", "ss", "m", "mq", "v", "lv", "a", "b"):
                self.at(f"{pre}_{nm}", [128, ncols], F32)
        eqb = self.at("eqb", [128, NE], F32)
        prefb = self.at("prefb", [128, NE], F32)
        epsc = self.at("epsc", [128, 1], F32)
        self.epsc = epsc
        p.op("pool", lambda e: e.memset(epsc[:], EPS), writes=["epsc"])
        scr = self.at("scr", [128, C], F32)
        x1b = [self.at(f"x1b{i}", [128, C], F32) for i in range(2)]
        ln3b = self.at("ln3b", [128, C], F32)
        bo_bf = self.at("bo_bf", [1, C], BF16)
        b_bcast = self.at("b_bcast", [128, C], F32)
        tmpf = [self.at(f"tmpf{i}", [128, 128], F32) for i in range(2)]

        self.rbnd = {}

        def _regs(e):
            r1 = e.alloc_register("bndg")
            e.reg_mov(r1, TOK - 1)
            r2 = e.alloc_register("bnds")
            inst = e.reg_mov(r2, NE * CAP - 1)
            self.rbnd["g"] = r1
            self.rbnd["s"] = r2
            return inst

        p.op("pool", _regs)
        p.op("pool", lambda e: e.memset(ident[:], 0.0), writes=["ident"])
        p.op("pool", lambda e: e.affine_select(out=ident[:], in_=ident[:],
                                               compare_op=ALU.not_equal, fill=1.0,
                                               base=0, pattern=[[-1, 128]],
                                               channel_multiplier=1),
             reads=["ident"], writes=["ident"])
        p.op("pool", lambda e: e.memset(ones_bf[:], 1.0), writes=["ones"])
        p.op("pool", lambda e: e.memset(ones8[:], 1.0), writes=["ones8"])
        p.op("qsp", lambda e: e.dma_start(
            out=gw_sb[:].rearrange("p (dc e) -> p dc e", dc=NDC),
            in_=self.gwT[:].rearrange("(dc p) e -> p dc e", p=128)),
            writes=["gw"])
        p.op("qpool", lambda e: e.dma_start(out=bo_bf[:], in_=self.bo[:]),
             writes=["bo_bf"])
        for j in range(2):
            ap, keys = self.pb(512 * j, 512 * j + 512)
            p.op("pe", lambda e, ap=ap, j=j: e.matmul(
                ap, ones_bf[:], bo_bf[:, 512 * j:512 * j + 512], start=True, stop=True),
                reads=["ones", "bo_bf"], writes=keys)
        ap, keys = self.pb(0, 1024)
        p.op("dve", lambda e, ap=ap: e.tensor_copy(b_bcast[:], ap),
             reads=keys, writes=["b_bcast"])

        ln1_fm = [self.at(f"ln1_fm{dc}", [128, S], BF16) for dc in range(NDC)]
        xt = [self.at(f"xt{i}", [128, C], F32) for i in range(2)]
        p0b = [self.at(n, [128, C], F32)
               for n in ("p0q0", "p0q1", "p0k0a", "p0k0b")]
        xpool = [(xt[0], "xt0"), (xt[1], "xt1"), (x1b[0], "x1b0"), (x1b[1], "x1b1"),
                 (p0b[0], "p0q0"), (p0b[1], "p0q1"),
                 (p0b[2], "p0k0a"), (p0b[3], "p0k0b")]
        def p0_s1(tt):
            xb_, xk_ = xpool[tt % 8]
            p.op("qsp", lambda e, tt=tt, xb_=xb_: e.dma_start(
                out=xb_[:], in_=self.x_in[128 * tt:128 * tt + 128, :]),
                writes=[xk_], tag=f"ldx{tt}")
            s_ = t["L1_s"][:, tt:tt + 1]
            ss = t["L1_ss"][:, tt:tt + 1]
            p.op("dve", lambda e, xb_=xb_, s_=s_: e.reduce_sum(s_, xb_[:], axis=AX.X),
                 reads=[xk_], writes=[f"L1_s{tt}"], tag=f"L1{tt}_rs")
            p.op("act", lambda e, xb_=xb_, ss=ss: e.activation(
                scr[:], xb_[:], AF.Square, accum_out=ss),
                reads=[xk_], writes=[f"L1_ss{tt}", "scr"], tag=f"L1{tt}_sq")

        def p0_s2(tt):
            s_ = t["L1_s"][:, tt:tt + 1]
            ss = t["L1_ss"][:, tt:tt + 1]
            m_ = t["L1_m"][:, tt:tt + 1]
            mq = t["L1_mq"][:, tt:tt + 1]
            v_ = t["L1_v"][:, tt:tt + 1]
            p.op("dve", lambda e, s_=s_, m_=m_: e.tensor_single_scalar(m_, s_, 1.0 / C, ALU.mult),
                 reads=[f"L1_s{tt}"], writes=[f"L1_m{tt}"])
            p.op("dve", lambda e, m_=m_, mq=mq: e.tensor_mul(mq, m_, m_),
                 reads=[f"L1_m{tt}"], writes=[f"L1_mq{tt}"])
            p.op("dve", lambda e, ss=ss, mq=mq, v_=v_: e.tensor_scalar(
                v_, ss, 1.0 / C, mq, ALU.mult, ALU.subtract),
                reads=[f"L1_ss{tt}", f"L1_mq{tt}"], writes=[f"L1_v{tt}"])

        def p0_s3(tt):
            v_ = t["L1_v"][:, tt:tt + 1]
            lv = t["L1_lv"][:, tt:tt + 1]
            a_ = t["L1_a"][:, tt:tt + 1]
            p.op("act", lambda e, v_=v_, lv=lv: e.activation(lv, v_, AF.Ln, bias=self.epsc[:]),
                 reads=[f"L1_v{tt}", "epsc"], writes=[f"L1_lv{tt}"])
            p.op("act", lambda e, lv=lv, a_=a_: e.activation(a_, lv, AF.Exp, scale=-0.5),
                 reads=[f"L1_lv{tt}"], writes=[f"L1_a{tt}"])

        def p0_s4(tt):
            xb_, xk_ = xpool[tt % 8]
            m_ = t["L1_m"][:, tt:tt + 1]
            a_ = t["L1_a"][:, tt:tt + 1]
            b_ = t["L1_b"][:, tt:tt + 1]
            p.op("dve", lambda e, m_=m_, a_=a_, b_=b_: e.tensor_scalar(
                b_, m_, a_, -1.0, ALU.mult, ALU.mult),
                reads=[f"L1_m{tt}", f"L1_a{tt}"], writes=[f"L1_b{tt}"])
            p.op("dve", lambda e, xb_=xb_, a_=a_, b_=b_: e.tensor_scalar(
                xb_[:], xb_[:], a_, b_, ALU.mult, ALU.add),
                reads=[xk_, f"L1_a{tt}", f"L1_b{tt}"], writes=[xk_],
                tag=f"ln1app{tt}")
            for dc in range(NDC):
                bank = (tt * NDC + dc) % 4
                tp_ap, tp_keys = self.pa(512 * bank, 512 * bank + 128)
                p.op("pe", lambda e, dc=dc, xb_=xb_, tp_ap=tp_ap: e.transpose(
                    tp_ap, xb_[:, 128 * dc:128 * dc + 128], ident[:]),
                    reads=[xk_, "ident"], writes=tp_keys, tag=f"tr{tt}_{dc}")
                ech = "dve" if dc < 3 else "act"
                def _ev(e, dc=dc, tt=tt, tp_ap=tp_ap):
                    if hasattr(e, "tensor_copy"):
                        return e.tensor_copy(ln1_fm[dc][:, 128 * tt:128 * tt + 128], tp_ap)
                    return e.activation(ln1_fm[dc][:, 128 * tt:128 * tt + 128], tp_ap, AF.Copy)
                p.op(ech, _ev, reads=tp_keys, writes=[f"ln1fm{dc}_{tt}"],
                     tag=f"trev{tt}_{dc}")

        for i in range(NT_ALL + 3):
            if i < NT_ALL:
                p0_s1(i)
            if 0 <= i - 1 < NT_ALL:
                p0_s2(i - 1)
            if 0 <= i - 2 < NT_ALL:
                p0_s3(i - 2)
            if 0 <= i - 3 < NT_ALL:
                p0_s4(i - 3)

        O_fm = [self.at(f"O_fm{dc}", [128, TOK], BF16) for dc in range(NDC)]
        wq_sb = self.at("wq_sb", [128, NDC * GC], BF16)
        wk_sb = self.at("wk_sb", [128, NDC * GC], BF16)
        wv_sb = self.at("wv_sb", [128, NDC * GC], BF16)
        Q_sb = [self.at(f"Q_sb{m}", [128, TOK], BF16) for m in range(2)]
        K_sb = [self.at(f"K_sb{m}", [128, S], BF16) for m in range(2)]
        V_sb = self.at("V_sb", [128, NT_ALL * VW], BF16)
        A01 = [self.at(f"A01_{i}", [128, 1024], BF16) for i in range(2)]
        A23 = [self.at(f"A23_{i}", [128, 1024], BF16) for i in range(2)]
        srow = self.at("srow", [1, 512], F32)
        trow = self.at("trow", [1, 512], F32)
        rse_bf = self.at("rse_bf", [1, 512], BF16)
        trow1 = self.nc.alloc_sbuf_tensor_at("trow1", [1, 512], F32,
                                             offset=_L["srow"])
        rse_bf1 = self.nc.alloc_sbuf_tensor_at("rse_bf1", [1, 512], BF16,
                                               offset=_L["A23_1"])
        trow_h = [(trow, "trow"), (trow1, "srow")]
        rse_h = [(rse_bf, "rsebf"), (rse_bf1, "rsebf1")]
        rse_t = [self.at(f"rse_t{i}", [64, 512], F32) for i in range(2)]

        p.op("pool", lambda e: e.memset(V_sb[:], 0.0),
             writes=[f"V{tt}" for tt in range(NT_ALL)], tag="vzero")
        for tt in range(NT_OWN):
            p.op("qpool", lambda e, tt=tt: e.dma_start(
                out=self.rout_sp[128 * tt:128 * tt + 128, :], in_=V_sb[:, 0:C]),
                reads=["V0", "V1", "V2", "V3"], writes=[f"rz{tt}"], tag=f"rz{tt}")
        p.op("pool", lambda e: e.memset(V_sb[:], 1.0),
             writes=[f"V{tt}" for tt in range(NT_ALL)], tag="vones")

        def emit_precast(g):
            if g == 0:
                for half, srcT in ((0, self.wsgT), (1, self.wsuT)):
                    for dcc in range(0, C, 256):
                        p.op("qpool", lambda e, srcT=srcT, half=half, dcc=dcc: e.dma_start(
                            out=self.wsgu_bf[dcc:dcc + 256, :, 128 * half:128 * half + 128],
                            in_=srcT[dcc:dcc + 256, :].rearrange("c (f k) -> c f k", f=NFC)),
                            writes=["wbfsgu"], tag=f"pcs{half}_{dcc}")
                p.op("qpool", lambda e: e.dma_start(
                    out=self.wsd_bf[:], in_=self.wsdT[:]),
                    writes=["wbfsd"], tag="pcsd")
            for ei in (2 * g, 2 * g + 1):
                for half, srcT in ((0, self.wrgT), (1, self.wruT)):
                    for dcc in range(0, C, 256):
                        p.op("qpool", lambda e, srcT=srcT, half=half, ei=ei, dcc=dcc: e.dma_start(
                            out=self.wrgu_bf[ei][dcc:dcc + 256, :, 128 * half:128 * half + 128],
                            in_=srcT[ei][dcc:dcc + 256, :].rearrange("c (f k) -> c f k", f=NFC)),
                            writes=[f"wbfgu{ei}"], tag=f"pcgu{ei}_{half}_{dcc}")
                p.op("qpool", lambda e, ei=ei: e.dma_start(
                    out=self.wrd_bf[ei], in_=self.wrdT[ei]),
                    writes=[f"wbfrd{ei}"], tag=f"pcrd{ei}")

        for g in range(NGROUP):
            for nm, sb_, base in (("q", wq_sb, 0), ("k", wk_sb, C), ("v", wv_sb, 2 * C)):
                p.op("qpool", lambda e, sb_=sb_, base=base, g=g: e.dma_start(
                    out=sb_[:].rearrange("p (dc f) -> p dc f", dc=NDC),
                    in_=self.wqT[:, base + GC * g: base + GC * g + GC]
                        .rearrange("(dc p) f -> p dc f", p=128)),
                    writes=[f"w{nm}"], tag=f"ldw{nm}{g}")
            emit_precast(g)
            for nm, sb_w, dst, ntb in (("q", wq_sb, Q_sb, 4), ("k", wk_sb, K_sb, 8)):
                for m2 in range(2):
                    for tb in range(ntb):
                        qp, qk = self.pb(512 * (tb % 4), 512 * (tb % 4) + 512)
                        for dc in range(NDC):
                            p.op("pe", lambda e, m2=m2, tb=tb, dc=dc, qp=qp, sb_w=sb_w: e.matmul(
                                qp, sb_w[:, GC * dc + 128 * m2: GC * dc + 128 * m2 + 128],
                                ln1_fm[dc][:, 512 * tb:512 * tb + 512],
                                start=(dc == 0), stop=(dc == NDC - 1)),
                                reads=[f"w{nm}"] + [f"ln1fm{dc}_{i}" for i in range(4 * tb, 4 * tb + 4)],
                                writes=qk, tag=f"{nm}mm{g}_{m2}_{tb}")
                        if nm == "q":
                            guard = [f"p0q{m2}"] if g == 0 else []
                            p.op("dve", lambda e, m2=m2, tb=tb, qp=qp, dst=dst: e.tensor_single_scalar(
                                dst[m2][:, 512 * tb:512 * tb + 512], qp, QSCALE, ALU.mult),
                                reads=qk, writes=[f"Q{m2}"] + guard, tag=f"qev{g}_{m2}_{tb}")
                        else:
                            guard = ([f"p0k0{'a' if tb < 4 else 'b'}"]
                                     if (g == 0 and m2 == 0) else [])
                            p.op("dve", lambda e, m2=m2, tb=tb, qp=qp, dst=dst: e.tensor_copy(
                                dst[m2][:, 512 * tb:512 * tb + 512], qp),
                                reads=qk, writes=[f"K{m2}"] + guard, tag=f"kev{g}_{m2}_{tb}")
            for tt in range(NT_ALL):
                vp, vk = self.pa(1024 + 512 * (tt % 2), 1024 + 512 * (tt % 2) + GC)
                for dc in range(NDC):
                    p.op("pe", lambda e, tt=tt, dc=dc, vp=vp: e.matmul(
                        vp, ln1_fm[dc][:, 128 * tt:128 * tt + 128],
                        wv_sb[:, GC * dc:GC * dc + GC],
                        start=(dc == 0), stop=(dc == NDC - 1)),
                        reads=["wv", f"ln1fm{dc}_{tt}"], writes=vk, tag=f"vmm{g}_{tt}")
                p.op("dve", lambda e, tt=tt, vp=vp: e.tensor_copy(
                    V_sb[:, VW * tt:VW * tt + VW]
                        .rearrange("p (j s) -> p j s", j=GROUP_HEADS, s=HD + 1)[:, :, 0:HD],
                    vp.rearrange("p (j s) -> p j s", j=GROUP_HEADS, s=HD)),
                    reads=vk, writes=[f"V{tt}"], tag=f"vev{g}_{tt}")

            def emit_S(kc, qb, m2, hpairbase):
                kpar = kc % 2
                for hb in range(2):
                    sp_ap, sp_k = self.pb(1024 * kpar + 512 * hb,
                                          1024 * kpar + 512 * hb + 512)
                    p.op("pe", lambda e, sp_ap=sp_ap, m2=m2, hb=hb, kc=kc, qb=qb: e.matmul(
                        sp_ap, K_sb[m2][64 * hb:64 * hb + 64, 128 * kc:128 * kc + 128],
                        Q_sb[m2][64 * hb:64 * hb + 64, 512 * qb:512 * qb + 512],
                        start=True, stop=True),
                        reads=[f"K{m2}", f"Q{m2}"], writes=sp_k,
                        tag=f"S{hpairbase}_{hb}_{kc}_{qb}")

            prefilled = [False]
            for m2 in range(2):
                hpairbase = GROUP_HEADS * g + 2 * m2
                dcO = (hpairbase) // 2
                for qb in range(4):
                    qpar = qb % 2
                    if not prefilled[0]:
                        emit_S(0, qb, m2, hpairbase)
                        emit_S(1, qb, m2, hpairbase)
                    prefilled[0] = False
                    Abufs = [A01[0], A01[1], A23[0]]
                    for kc in range(NT_ALL):
                        kpar = kc % 2
                        apar = kc % 3
                        e_ap, e_k = self.pb(1024 * kpar, 1024 * kpar + 1024)
                        Abuf = Abufs[apar]
                        p.op("act", lambda e, e_ap=e_ap, Abuf=Abuf: e.activation(
                            Abuf[:], e_ap, AF.Exp),
                            reads=e_k, writes=[f"A{apar}"],
                            tag=f"E{hpairbase}_{qb}_{kc}")
                        if kc + 2 < NT_ALL:
                            emit_S(kc + 2, qb, m2, hpairbase)
                        for hb in range(2):
                            j = 2 * m2 + hb
                            av_ap, av_k = self.pa(1024 * qpar + 512 * hb,
                                                  1024 * qpar + 512 * hb + 512, 0, 65)
                            p.op("pe", lambda e, av_ap=av_ap, Abuf=Abuf, kc=kc, hb=hb, j=j: e.matmul(
                                av_ap, V_sb[:, VW * kc + (HD + 1) * j: VW * kc + (HD + 1) * j + (HD + 1)],
                                Abuf[:, 512 * hb:512 * hb + 512],
                                start=(kc == 0), stop=(kc == NT_ALL - 1)),
                                reads=[f"V{kc}", f"A{apar}"], writes=av_k,
                                tag=f"AV{hpairbase}_{hb}_{kc}_{qb}")
                    nqb, nm2 = (qb + 1, m2) if qb < 3 else (0, m2 + 1)
                    if nm2 < 2:
                        nhp = GROUP_HEADS * g + 2 * nm2
                        emit_S(0, nqb, nm2, nhp)
                        emit_S(1, nqb, nm2, nhp)
                        prefilled[0] = True
                    for hb in range(2):
                        h = hpairbase + hb
                        epar = hb
                        tr_, trk = trow_h[hb]
                        rs_, rsk = rse_h[hb]
                        av_full, avf_k = self.pa(1024 * qpar + 512 * hb,
                                                 1024 * qpar + 512 * hb + 512)
                        p.op("act", lambda e, av_full=av_full, tr_=tr_: e.activation(
                            tr_[:], av_full[64:65, :], AF.Ln),
                            reads=avf_k, writes=[trk], tag=f"se{h}_{qb}")
                        p.op("act", lambda e, tr_=tr_, rs_=rs_: e.activation(
                            rs_[:], tr_[:], AF.Exp, scale=-1.0),
                            reads=[trk], writes=[rsk])
                        bc_ap, bc_k = self.pa(1024 * (1 - qpar) + 512 * hb,
                                              1024 * (1 - qpar) + 512 * hb + 512, 0, 64)
                        p.op("pe", lambda e, bc_ap=bc_ap, rs_=rs_: e.matmul(
                            bc_ap, ones_bf[:, 0:64], rs_[:], start=True, stop=True),
                            reads=["ones", rsk], writes=bc_k, tag=f"bc{h}_{qb}")
                        p.op("dve", lambda e, bc_ap=bc_ap, epar=epar: e.tensor_copy(
                            rse_t[epar][:], bc_ap),
                            reads=bc_k, writes=[f"rset{epar}"])
                        av_o, avo_k = self.pa(1024 * qpar + 512 * hb,
                                              1024 * qpar + 512 * hb + 512, 0, 64)
                        offO = 64 * (h % 2)
                        p.op("dve", lambda e, av_o=av_o, epar=epar, dcO=dcO, offO=offO, qb=qb: e.tensor_tensor(
                            O_fm[dcO][offO:offO + 64, 512 * qb:512 * qb + 512],
                            av_o, rse_t[epar][:], ALU.mult),
                            reads=avo_k + (f"rset{epar}",), writes=[f"O{dcO}_{qb}"],
                            tag=f"On{h}_{qb}")

        fm_buf = [self.at(f"fm{dc}", [128, TOK], BF16) for dc in range(NDC)]
        wo_sb = self.at("wo_sb", [128, NDC * C], BF16)
        xo = [self.at(f"xo{i}", [128, C], F32) for i in range(2)]
        tb1 = [self.at(f"tb1_{i}", [128, C], F32) for i in range(2)]
        aob = [self.at(f"aob{i}", [128, C], F32) for i in range(2)]
        zb = [self.at(f"zb{i}", [128, C], F32) for i in range(2)]
        ln3c = [self.at(f"ln3c{i}", [128, C], F32) for i in range(2)]
        p.op("qpool", lambda e: e.dma_start(
            out=wo_sb[:].rearrange("p (dc f) -> p dc f", dc=NDC),
            in_=self.woT[:].rearrange("(dc p) f -> p dc f", p=128)),
            writes=["wo"] + [f"ln1fm{dc}_{i}" for dc in (4, 5) for i in range(NT_ALL)],
            tag="ldwo")

        def p4a_mms(tt):
            par = tt % 2
            xo_guard = [f"ln1fm6_{i}" for i in range(NT_ALL)] if tt < 2 else []
            p.op("qsp", lambda e, tt=tt, par=par: e.dma_start(
                out=xo[par][:], in_=self.x_in[128 * tt:128 * tt + 128, :]),
                writes=[f"xo{par}"] + xo_guard, tag=f"ldxo{tt}")
            for jh in range(2):
                for dc in range(NDC):
                    p.op("pe", lambda e, jh=jh, dc=dc, tt=tt, par=par: e.matmul(
                        self.ps_big[:, 1024 * par + 512 * jh:1024 * par + 512 * jh + 512],
                        O_fm[dc][:, 128 * tt:128 * tt + 128],
                        wo_sb[:, C * dc + 512 * jh:C * dc + 512 * jh + 512],
                        start=(dc == 0), stop=(dc == NDC - 1)),
                        reads=["wo"] + [f"O{dc}_{qb}" for qb in range(4)],
                        writes=[f"pb{2 * par + jh}"], tag=f"op{tt}_{jh}")

        def p4a_chain(tt):
            par = tt % 2
            op_ap, op_k = self.pb(1024 * par, 1024 * par + 1024)
            a1 = t["L1_a"][:, tt:tt + 1]
            b1 = t["L1_b"][:, tt:tt + 1]
            p.op("dve", lambda e, par=par, a1=a1, b1=b1: e.tensor_scalar(
                tb1[par][:], xo[par][:], a1, b1, ALU.mult, ALU.add),
                reads=[f"xo{par}", f"L1_a{tt}", f"L1_b{tt}"], writes=[f"tb1{par}"],
                tag=f"ln1r{tt}")
            p.op("dve", lambda e, par=par: e.tensor_add(tb1[par][:], tb1[par][:], b_bcast[:]),
                 reads=[f"tb1{par}", "b_bcast"], writes=[f"tb1{par}"])
            p.op("dve", lambda e, op_ap=op_ap, par=par: e.tensor_add(aob[par][:], tb1[par][:], op_ap),
                 reads=[f"tb1{par}"] + list(op_k), writes=[f"aob{par}"], tag=f"ao{tt}")
            a2, b2, abk2 = self.ln_stats(aob[par][:], [f"aob{par}"], "L2", tt, scr[:], "scr")
            self.ln_apply(zb[par][:], [f"zb{par}"], aob[par][:], [f"aob{par}"], a2, b2, abk2,
                          tag=f"ln2a{tt}")
            p.op("dve", lambda e, par=par: e.tensor_add(x1b[par][:], xo[par][:], zb[par][:]),
                 reads=[f"xo{par}", f"zb{par}"], writes=[f"x1b{par}"], tag=f"x1{tt}")
            p.op("qsp", lambda e, tt=tt, par=par: e.dma_start(
                out=self.x1sp[128 * tt:128 * tt + 128, :], in_=x1b[par][:]),
                reads=[f"x1b{par}"], writes=[f"x1sp{tt}"], tag=f"stx1{tt}")
            a3, b3, abk3 = self.ln_stats(x1b[par][:], [f"x1b{par}"], "L3", tt,
                                         scr[:], "scr")
            self.ln_apply(ln3c[par][:], [f"ln3c{par}"], x1b[par][:], [f"x1b{par}"], a3, b3, abk3,
                          tag=f"ln3a{tt}")

        def p4a_gates(tt):
            par = tt % 2
            g_ap, g_k = self.pa(1536, 1536 + NE)
            for dc in range(NDC):
                tpar = dc % 2
                tp_ap, tp_k = self.pa(512 * tpar, 512 * tpar + 128)
                p.op("pe", lambda e, dc=dc, par=par, tp_ap=tp_ap: e.transpose(
                    tp_ap, ln3c[par][:, 128 * dc:128 * dc + 128], ident[:]),
                    reads=[f"ln3c{par}", "ident"], writes=tp_k, tag=f"tr3_{tt}_{dc}")
                p.op("act", lambda e, tp_ap=tp_ap, tpar=tpar: e.activation(
                    tmpf[tpar][:], tp_ap, AF.Copy),
                    reads=tp_k, writes=[f"tmpf{tpar}"], tag=f"tr3e_{tt}_{dc}")
                p.op("act", lambda e, dc=dc, tt=tt, tp_ap=tp_ap: e.activation(
                    fm_buf[dc][:, 128 * tt:128 * tt + 128], tp_ap, AF.Copy),
                    reads=tp_k, writes=[f"fm{dc}_{tt}"], tag=f"cast3_{tt}_{dc}")
                p.op("pe", lambda e, dc=dc, g_ap=g_ap, tpar=tpar: e.matmul(
                    g_ap, tmpf[tpar][:], gw_sb[:, NE * dc:NE * dc + NE],
                    start=(dc == 0), stop=(dc == NDC - 1)),
                    reads=[f"tmpf{tpar}", "gw"], writes=g_k, tag=f"gate{tt}_{dc}")

        def p4a_lgev(tt):
            g_ap, g_k = self.pa(1536, 1536 + NE)
            lg = logits_all[:, NE * tt:NE * tt + NE]
            p.op("act", lambda e, g_ap=g_ap, lg=lg: e.activation(lg, g_ap, AF.Copy),
                 reads=g_k, writes=[f"lg{tt}"], tag=f"lgev{tt}")
            p.op("dve", lambda e, lg=lg, tt=tt: e.reduce_max(
                rmax_all[:, tt:tt + 1], lg, axis=AX.X),
                reads=[f"lg{tt}"], writes=[f"rmax{tt}"])

        p4a_mms(0)
        p4a_chain(0)
        for tt in range(NT_OWN):
            if tt + 1 < NT_OWN:
                p4a_mms(tt + 1)
            p4a_gates(tt)
            if tt + 1 < NT_OWN:
                p4a_chain(tt + 1)
            p4a_lgev(tt)

        p.op("act", lambda e: e.activation(scores[:], rmax_all[:], AF.Sigmoid),
             reads=[f"rmax{tt}" for tt in range(NT_OWN)], writes=["scores"],
             tag="sigmoid")
        p.op("dve", lambda e: e.tensor_mul(alpha[:], t["L3_a"][:], scores[:]),
             reads=["scores"] + [f"L3_a{tt}" for tt in range(NT_OWN)], writes=["alpha"])
        p.op("dve", lambda e: e.tensor_mul(beta[:], t["L3_b"][:], scores[:]),
             reads=["scores"] + [f"L3_b{tt}" for tt in range(NT_OWN)], writes=["beta"])

        def emit_masks():
            for tt in range(NT_OWN):
                lg = logits_all[:, NE * tt:NE * tt + NE]
                p.op("dve", lambda e, lg=lg, tt=tt: e.tensor_scalar(
                        eqb[:], lg, rmax_all[:, tt:tt + 1], None, ALU.is_equal),
                        reads=[f"lg{tt}", f"rmax{tt}"], writes=["eqb"], tag=f"eq{tt}")
                p.op("dve", lambda e: e.tensor_tensor_scan(
                        prefb[:], eqb[:], ones8[:], 0.0, ALU.add, ALU.mult),
                        reads=["eqb", "ones8"], writes=["prefb"], tag=f"scan{tt}")
                p.op("dve", lambda e: e.tensor_single_scalar(
                        prefb[:], prefb[:], 1.0, ALU.is_equal),
                        reads=["prefb"], writes=["prefb"])
                p.op("dve", lambda e, tt=tt: e.tensor_mul(
                        mask_all[:, NE * tt:NE * tt + NE], prefb[:], eqb[:]),
                        reads=["prefb", "eqb"], writes=[f"mask{tt}"], tag=f"mask{tt}")


        Ltri = self.at("Ltri", [128, 128], BF16)
        mask_em = self.at("mask_em", [128, NE * NT_OWN], F32)
        c_all = self.at("c_all", [128, NE * NT_OWN], F32)
        ones16 = self.at("ones16", [128, NT_OWN], F32)
        T8b = self.at("T8b", [128, NE], BF16)
        Pexcl = self.at("Pexcl", [128, NE], F32)
        slotg = self.at("slotg", [128, NT_OWN], F32)
        slotu = self.at("slotu", [128, NT_OWN], U32)
        selm = self.at("selm", [128, NT_OWN], F32)
        selmu = self.at("selmu", [128, NT_OWN], U8)
        t1b = self.at("t1b", [128, NT_OWN], F32)
        tok_iota = self.at("tok_iota", [128, NT_OWN], U32)
        pref_sb = self.at("pref_sb", [128, 32], U32)
        idx_sb = self.at("idx_sb", [128, 32], U32)

        def emit_routing_head():
            p.op("pool", lambda e: e.memset(Ltri[:], 0.0), writes=["Ltri"])
            p.op("pool", lambda e: e.affine_select(out=Ltri[:], in_=Ltri[:],
                                                   compare_op=ALU.is_ge, fill=1.0,
                                                   base=0, pattern=[[-1, 128]],
                                                   channel_multiplier=1),
                 reads=["Ltri"], writes=["Ltri"])
            p.op("pool", lambda e: e.memset(ones16[:], 1.0), writes=["ones16"])
            p.op("pool", lambda e: e.memset(slotg[:], 1e9), writes=["slotg"])
            p.op("pool", lambda e: e.iota(tok_iota[:], pattern=[[128, NT_OWN]],
                                          base=0, channel_multiplier=1),
                 writes=["toki"])
            p.op("pool", lambda e: e.iota(
                pref_sb[:], pattern=[[128, 32]], base=TOK, channel_multiplier=1),
                writes=["prefsb"])
            mview = mask_all[:].rearrange("p (t e2) -> p t e2", t=NT_OWN)
            for ei in range(NE):
                p.op("dve", lambda e, ei=ei: e.tensor_copy(
                    mask_em[:, NT_OWN * ei:NT_OWN * ei + NT_OWN], mview[:, :, ei]),
                    reads=[f"mask{tt}" for tt in range(NT_OWN)], writes=["mem"],
                    tag=f"mem{ei}")
                p.op("dve", lambda e, ei=ei: e.tensor_tensor_scan(
                    c_all[:, NT_OWN * ei:NT_OWN * ei + NT_OWN],
                    mask_em[:, NT_OWN * ei:NT_OWN * ei + NT_OWN],
                    ones16[:], 0.0, ALU.add, ALU.mult),
                    reads=["mem", "ones16"], writes=["call"], tag=f"cscan{ei}")
            cview = c_all[:].rearrange("p (e2 t) -> p e2 t", e2=NE)
            p.op("dve", lambda e: e.tensor_copy(T8b[:], cview[:, :, NT_OWN - 1]),
                 reads=["call"], writes=["T8b"])
        def emit_routing_tail():
            px_ap, px_k = self.pa(1536, 1536 + NE)
            p.op("pe", lambda e: e.matmul(px_ap, Ltri[:], T8b[:], start=True, stop=True),
                 reads=["Ltri", "T8b"], writes=px_k, tag="prefmm")
            p.op("dve", lambda e: e.tensor_copy(Pexcl[:], px_ap), reads=px_k,
                 writes=["Pexcl"])
            for ei in range(NE):
                ce = c_all[:, NT_OWN * ei:NT_OWN * ei + NT_OWN]
                me = mask_em[:, NT_OWN * ei:NT_OWN * ei + NT_OWN]
                p.op("dve", lambda e, ce=ce, ei=ei: e.tensor_scalar(
                    t1b[:], ce, Pexcl[:, ei:ei + 1], float(CAP * ei - 1), ALU.add, ALU.add),
                    reads=["call", "Pexcl"], writes=["t1b"], tag=f"slot{ei}")
                p.op("dve", lambda e, ei=ei: e.tensor_single_scalar(
                    selm[:], t1b[:], float(CAP * (ei + 1)), ALU.is_lt),
                    reads=["t1b"], writes=["selm"])
                p.op("dve", lambda e, me=me: e.tensor_mul(selm[:], selm[:], me),
                     reads=["selm", "mem"], writes=["selm"])
                p.op("dve", lambda e: e.tensor_copy(selmu[:], selm[:]),
                     reads=["selm"], writes=["selmu"])
                p.op("dve", lambda e: e.select(slotg[:], selmu[:], t1b[:], slotg[:],
                                               add_drain=True),
                     reads=["selmu", "t1b", "slotg"], writes=["slotg"], tag=f"ssel{ei}")
            p.op("dve", lambda e: e.tensor_copy(slotu[:], slotg[:]),
                 reads=["slotg"], writes=["slotu"])

        def emit_prefill():
            for k in range(NSLOT):
                p.op("qpool", lambda e, k=k: e.dma_start(
                    out=self.idx_dram[128 * k:128 * k + 128, :], in_=pref_sb[:, k:k + 1]),
                    reads=["prefsb"], writes=[f"idxd{k}"], tag=f"pfill{k}")
        def emit_index_build():
            pf_keys = [f"idxd{k}" for k in range(NSLOT)]
            for tt in range(NT_OWN):
                p.op("qpool", lambda e, tt=tt: e.indirect_dma_start(
                    out=self.idx_dram[:], out_offset=IndirectOffsetOnAxis(slotu[:, tt:tt + 1], 0),
                    in_=tok_iota[:, tt:tt + 1], in_offset=None,
                    bounds_check=self.rbnd["s"], oob_is_err=False),
                    reads=["slotu", "toki"] + pf_keys, writes=[f"isc{tt}"], tag=f"iscat{tt}")
            is_keys = [f"isc{tt}" for tt in range(NT_OWN)]
            for k in range(NSLOT):
                p.op("qpool", lambda e, k=k: e.dma_start(
                    out=idx_sb[:, k:k + 1], in_=self.idx_dram[128 * k:128 * k + 128, :]),
                    reads=[f"idxd{k}"] + is_keys, writes=[f"idx{k}"], tag=f"ldidx{k}")

        ffn_acc = self.at("ffn_acc", [128, NT_OWN * C], F32)
        pe_half = self.at("pe_half", [128, NFC * 1024], BF16)
        wgu_b = [self.at(f"wgu_b{i}", [128, NDC * 256], BF16) for i in range(2)]
        wd_b = [self.at(f"wd_b{i}", [128, NFC * 256], BF16) for i in range(2)]
        silb = [self.at(f"silb{i}", [128, 1024], BF16) for i in range(2)]
        guards = {
            "wd_b0": [f"O{dc}_{qb}" for dc in (0, 1) for qb in range(4)],
            "wd_b1": [f"O{dc}_{qb}" for dc in (2, 3) for qb in range(4)],
            "wgu_b0": [f"O{dc}_{qb}" for dc in (4, 5) for qb in range(4)],
            "wgu_b1": [f"O{dc}_{qb}" for dc in (4, 5) for qb in range(4)],
            "sil0": [f"O6_{qb}" for qb in range(4)],
            "sil1": [f"O6_{qb}" for qb in range(4)],
        }

        xsb = [(ln3b, "ln3b"), (scr, "scr")]

        def p4b_tile(tt):
            par = tt % 2
            p.op("qpool", lambda e, tt=tt, par=par: e.dma_start(
                out=x1b[par][:], in_=self.x1sp[128 * tt:128 * tt + 128, :]),
                reads=[f"x1sp{tt}"], writes=[f"x1b{par}"], tag=f"ldx1b{tt}")
            xb_, xkey = xsb[par]
            p.op("dve", lambda e, par=par, tt=tt, xb_=xb_: e.tensor_scalar(
                xb_[:], x1b[par][:], alpha[:, tt:tt + 1], beta[:, tt:tt + 1],
                ALU.mult, ALU.add),
                reads=[f"x1b{par}", "alpha", "beta"], writes=[xkey], tag=f"xs{tt}")
            p.op("qpool", lambda e, tt=tt, xb_=xb_: e.dma_start(
                out=self.xssp[128 * tt:128 * tt + 128, :], in_=xb_[:]),
                reads=[xkey], writes=["xssp"], tag=f"stxs{tt}")

        def ffn_expert(e_idx, mid_hook=None):
            nm = "sh" if e_idx < 0 else f"e{e_idx}"
            gusrc, guk_ = self.wsgu_bf, "wbfsgu"
            dsrc, dk_ = self.wsd_bf, "wbfsd"
            for hh in range(2):
                if hh == 1 and mid_hook is not None:
                    mid_hook()
                for f in range(NFC):
                    wpar = f % 2
                    gk = guards.pop(f"wgu_b{wpar}", [])
                    p.op("qsp", lambda e, f=f, wpar=wpar: e.dma_start(
                        out=wgu_b[wpar][:].rearrange("p (dc f2) -> p dc f2", dc=NDC),
                        in_=gusrc[:, f, :].rearrange("(dc p) f2 -> p dc f2", p=128)),
                        reads=[guk_], writes=[f"wgu{wpar}"] + gk,
                        tag=f"ldgu{nm}_{hh}_{f}")
                    hpar = f % 2
                    zg_ap, zg_k = (self.pb if hpar == 0 else self.pa)(0, 1024)
                    zu_ap, zu_k = (self.pb if hpar == 0 else self.pa)(1024, 2048)
                    zap = self.ps_big if hpar == 0 else self.ps_av
                    for sub in range(2):
                        tok0 = 1024 * hh + 512 * sub
                        for half, zk, cof in ((0, zg_k, 0), (1, zu_k, 1024)):
                            for dc in range(NDC):
                                p.op("pe", lambda e, half=half, dc=dc, tok0=tok0, zap=zap, cof=cof, sub=sub, wpar=wpar: e.matmul(
                                    zap[:, cof + 512 * sub:cof + 512 * sub + 512],
                                    wgu_b[wpar][:, 256 * dc + 128 * half:256 * dc + 128 * half + 128],
                                    fm_buf[dc][:, tok0:tok0 + 512],
                                    start=(dc == 0), stop=(dc == NDC - 1)),
                                    reads=[f"wgu{wpar}"] + [f"fm{dc}_{i}" for i in range(tok0 // 128, tok0 // 128 + 4)],
                                    writes=[zk[sub]], tag=f"z{half}mm{nm}_{hh}_{f}_{sub}")
                    gk = guards.pop(f"sil{hpar}", [])
                    p.op("act", lambda e, zg_ap=zg_ap, hpar=hpar: e.activation(
                        silb[hpar][:], zg_ap, AF.Silu),
                        reads=zg_k, writes=[f"sil{hpar}"] + gk, tag=f"sil{nm}_{hh}_{f}")
                    p.op("dve", lambda e, zu_ap=zu_ap, hpar=hpar, f=f: e.tensor_tensor(
                        pe_half[:, 1024 * f:1024 * f + 1024],
                        silb[hpar][:], zu_ap, ALU.mult),
                        reads=[f"sil{hpar}"] + list(zu_k), writes=[f"pe{f}"],
                        tag=f"p{nm}_{hh}_{f}")
                    if e_idx < 0 and hh == 0:
                        p4b_tile(f)
                for jo in range(4):
                    jpar = jo % 2
                    gk = guards.pop(f"wd_b{jpar}", [])
                    p.op("qsp", lambda e, dsrc=dsrc, jo=jo, jpar=jpar: e.dma_start(
                        out=wd_b[jpar][:].rearrange("p (fc o) -> p fc o", fc=NFC),
                        in_=dsrc[:, 256 * jo:256 * jo + 256]
                            .rearrange("(fc p) o -> p fc o", p=128)),
                        reads=[dk_], writes=[f"wd{jpar}"] + gk, tag=f"ldd{nm}_{hh}_{jo}")
                    for t8 in range(8):
                        tt = 8 * hh + t8
                        dpar = t8 % 2
                        d_ap, d_k = self.pb(512 * dpar, 512 * dpar + 256)
                        for fc in range(NFC):
                            p.op("pe", lambda e, d_ap=d_ap, fc=fc, t8=t8, jpar=jpar: e.matmul(
                                d_ap, pe_half[:, 1024 * fc + 128 * t8: 1024 * fc + 128 * t8 + 128],
                                wd_b[jpar][:, 256 * fc:256 * fc + 256],
                                start=(fc == 0), stop=(fc == NFC - 1)),
                                reads=[f"wd{jpar}", f"pe{fc}"], writes=d_k,
                                tag=f"dmm{nm}_{hh}_{jo}_{t8}")
                        acc_ap = ffn_acc[:, C * tt + 256 * jo: C * tt + 256 * jo + 256]
                        p.op("dve", lambda e, d_ap=d_ap, acc_ap=acc_ap: e.tensor_copy(
                            acc_ap, d_ap),
                            reads=d_k, writes=[f"acc{tt}_{jo}"],
                            tag=f"dev{nm}_{hh}_{jo}_{t8}")

        emit_masks()
        emit_routing_head()
        emit_prefill()
        ffn_expert(-1, mid_hook=lambda: (emit_routing_tail(),
                                         emit_index_build()))

        xef = [self.at(f"xef{dc}", [128, CAP], BF16) for dc in range(NDC)]
        ident_bf = self.at("ident_bf", [128, 128], BF16)
        gbuf = [self.at(f"gbuf{i}", [128, C], BF16) for i in range(3)]
        p.op("pool", lambda e: e.tensor_copy(ident_bf[:], ident[:]),
             reads=["ident"], writes=["identbf", "srow", "trow", "rsebf",
                                      "rsebf1", "rset0", "rset1"])
        xbufs = [(x1b[0], "x1b0"), (x1b[1], "x1b1"), (ln3b, "ln3b"), (scr, "scr")]
        self._gi = 0

        def sp_gather(e_idx):
            nm = f"se{e_idx}"
            for k in range(CAPK):
                gi = self._gi % 3
                self._gi += 1
                buf, bkey = gbuf[gi], f"gb{gi}"
                icol = idx_sb[:, CAPK * e_idx + k: CAPK * e_idx + k + 1]
                p.op("qpool", lambda e, buf=buf, icol=icol: e.indirect_dma_start(
                    out=buf[:], out_offset=None, in_=self.xssp[:],
                    in_offset=IndirectOffsetOnAxis(icol, 0),
                    bounds_check=self.rbnd["g"], oob_is_err=False),
                    reads=[f"idx{CAPK * e_idx + k}", "xssp", bkey], writes=[bkey],
                    tag=f"gx{nm}_{k}")
                for dc in range(NDC):
                    tpar = dc % 2
                    tp_f, tp_k = self.pa(512 * tpar, 512 * tpar + 64)
                    tp_bf = tp_f.bitcast(BF16)
                    p.op("pe", lambda e, buf=buf, dc=dc, tp_bf=tp_bf: e.transpose(
                        tp_bf, buf[:, 128 * dc:128 * dc + 128], ident_bf[:]),
                        reads=[bkey, "identbf"], writes=tp_k, tag=f"tx{nm}_{k}_{dc}")
                    p.op("dve", lambda e, dc=dc, k=k, tp_bf=tp_bf: e.tensor_copy(
                        xef[dc][:, 128 * k:128 * k + 128], tp_bf),
                        reads=tp_k, writes=[f"xef{dc}"], tag=f"txe{nm}_{k}_{dc}")

        gub = [wgu_b[0][:], wgu_b[1][:],
               wd_b[0][:, 0:NDC * 256], wd_b[1][:, 0:NDC * 256]]

        def sp_w_dma(e_idx, f):
            nm = f"se{e_idx}"
            w4 = f % 4
            keys = [f"wgu{w4}"] + ([f"wd{w4 - 2}"] if w4 >= 2 else [])
            p.op("qsp", lambda e, e_idx=e_idx, f=f, w4=w4: e.dma_start(
                out=gub[w4].rearrange("p (dc f2) -> p dc f2", dc=NDC),
                in_=self.wrgu_bf[e_idx][:, f, :].rearrange("(dc p) f2 -> p dc f2", p=128)),
                reads=[f"wbfgu{e_idx}"], writes=keys, tag=f"ldgu{nm}_{f}")

        def sp_gu(e_idx, skip_w=0):
            nm = f"se{e_idx}"
            for f in range(NFC):
                wpar = f % 2
                w4 = f % 4
                if f >= skip_w:
                    sp_w_dma(e_idx, f)
                zg_ap, zg_k = self.pb(1024 * wpar, 1024 * wpar + CAP)
                zu_ap, zu_k = self.pb(1024 * wpar + 512, 1024 * wpar + 512 + CAP)
                for half, zap, zk in ((0, zg_ap, zg_k), (1, zu_ap, zu_k)):
                    for dc in range(NDC):
                        p.op("pe", lambda e, half=half, dc=dc, zap=zap, w4=w4: e.matmul(
                            zap, gub[w4][:, 256 * dc + 128 * half:256 * dc + 128 * half + 128],
                            xef[dc][:],
                            start=(dc == 0), stop=(dc == NDC - 1)),
                            reads=[f"wgu{w4}", f"xef{dc}"],
                            writes=zk, tag=f"z{half}mm{nm}_{f}")
                p.op("act", lambda e, zg_ap=zg_ap, wpar=wpar: e.activation(
                    silb[wpar][:, 0:CAP], zg_ap, AF.Silu),
                    reads=zg_k, writes=[f"sil{wpar}"], tag=f"sil{nm}_{f}")
                p.op("dve", lambda e, zu_ap=zu_ap, wpar=wpar, f=f: e.tensor_tensor(
                    pe_half[:, 1024 * f:1024 * f + CAP],
                    silb[wpar][:, 0:CAP], zu_ap, ALU.mult),
                    reads=[f"sil{wpar}"] + list(zu_k), writes=[f"pe{f}"],
                    tag=f"p{nm}_{f}")

        def sp_down(e_idx):
            nm = f"se{e_idx}"
            dsrc = self.wrd_bf[e_idx]
            for jo in range(4):
                jpar = jo % 2
                p.op("qsp", lambda e, dsrc=dsrc, jo=jo, jpar=jpar: e.dma_start(
                    out=wd_b[jpar][:].rearrange("p (fc o) -> p fc o", fc=NFC),
                    in_=dsrc[:, 256 * jo:256 * jo + 256]
                        .rearrange("(fc p) o -> p fc o", p=128)),
                    reads=[f"wbfrd{e_idx}"], writes=[f"wd{jpar}", f"wgu{jpar + 2}"],
                    tag=f"ldd{nm}_{jo}")
                for t8 in range(CAPK):
                    buf, bkey = xbufs[t8]
                    d_ap, d_k = self.pa(1024 + 512 * (t8 % 2),
                                        1024 + 512 * (t8 % 2) + 256)
                    for fc in range(NFC):
                        p.op("pe", lambda e, d_ap=d_ap, fc=fc, t8=t8, jpar=jpar: e.matmul(
                            d_ap, pe_half[:, 1024 * fc + 128 * t8: 1024 * fc + 128 * t8 + 128],
                            wd_b[jpar][:, 256 * fc:256 * fc + 256],
                            start=(fc == 0), stop=(fc == NFC - 1)),
                            reads=[f"wd{jpar}", f"pe{fc}"], writes=d_k,
                            tag=f"dmm{nm}_{jo}_{t8}")
                    p.op("dve", lambda e, d_ap=d_ap, buf=buf, jo=jo: e.tensor_copy(
                        buf[:, 256 * jo:256 * jo + 256], d_ap),
                        reads=d_k + (bkey,), writes=[bkey], tag=f"dev{nm}_{jo}_{t8}")

        rz_keys = [f"rz{tt}" for tt in range(NT_OWN)]

        def sp_scatter(e_idx):
            nm = f"se{e_idx}"
            for t8 in range(CAPK):
                buf, bkey = xbufs[t8]
                kk = CAPK * e_idx + t8
                icol = idx_sb[:, kk:kk + 1]
                p.op("qpool", lambda e, buf=buf, icol=icol: e.indirect_dma_start(
                    out=self.rout_sp[:], out_offset=IndirectOffsetOnAxis(icol, 0),
                    in_=buf[:], in_offset=None),
                    reads=[f"idx{kk}", bkey] + rz_keys, writes=[f"rs{kk}"],
                    tag=f"rs{nm}_{t8}")

        sp_gather(0)
        for e_idx in range(NE):
            sp_gu(e_idx, skip_w=2 if e_idx > 0 else 0)
            if e_idx + 1 < NE:
                sp_gather(e_idx + 1)
                sp_w_dma(e_idx + 1, 0)
                sp_w_dma(e_idx + 1, 1)
            sp_down(e_idx)
            sp_scatter(e_idx)

        for t0 in range(0, NT_OWN, 4):
            akeys = [f"acc{tt}_{jo}" for tt in range(t0, t0 + 4) for jo in range(4)]
            p.op("qpool", lambda e, t0=t0: e.dma_start(
                out=ffn_acc[:, C * t0:C * t0 + 4 * C]
                    .rearrange("p (a c) -> p a c", a=4),
                in_=self.rout_sp[128 * t0:128 * t0 + 512, :]
                    .rearrange("(a p) c -> p a c", p=128), accum_op=ALU.add),
                reads=[f"rs{kk}" for kk in range(NSLOT)] + akeys,
                writes=akeys, tag=f"raccum{t0}")
        p6x = [self.nc.alloc_sbuf_tensor_at(f"p6b{i}", [128, C], F32,
                                            offset=_L["pe_half"] + 4096 * i)
               for i in range(8)]
        p6buf = [(x1b[0], "x1b0"), (x1b[1], "x1b1")] + \
                [(p6x[i], f"p6b{i}") for i in range(8)]
        p6guard = {i + 2: [f"pe{2 * i}", f"pe{2 * i + 1}"] for i in range(8)}

        def p6_s1(tt):
            xb_, xk_ = p6buf[tt % 10]
            gk = p6guard.pop(tt, [])
            p.op("qsp", lambda e, tt=tt, xb_=xb_: e.dma_start(
                out=xb_[:], in_=self.x1sp[128 * tt:128 * tt + 128, :]),
                reads=[f"x1sp{tt}"], writes=[xk_] + gk, tag=f"ldx1c{tt}")
            f_ap = ffn_acc[:, C * tt:C * tt + C]
            fkeys = [f"acc{tt}_{jo}" for jo in range(4)]
            s_ = t["L4_s"][:, tt:tt + 1]
            ss = t["L4_ss"][:, tt:tt + 1]
            p.op("dve", lambda e, f_ap=f_ap, s_=s_: e.reduce_sum(s_, f_ap, axis=AX.X),
                 reads=fkeys, writes=[f"L4_s{tt}"], tag=f"L4{tt}_rs")
            p.op("act", lambda e, f_ap=f_ap, ss=ss: e.activation(
                scr[:], f_ap, AF.Square, accum_out=ss),
                reads=fkeys, writes=[f"L4_ss{tt}", "scr"], tag=f"L4{tt}_sq")

        def p6_s2(tt):
            s_ = t["L4_s"][:, tt:tt + 1]
            ss = t["L4_ss"][:, tt:tt + 1]
            m_ = t["L4_m"][:, tt:tt + 1]
            mq = t["L4_mq"][:, tt:tt + 1]
            v_ = t["L4_v"][:, tt:tt + 1]
            p.op("dve", lambda e, s_=s_, m_=m_: e.tensor_single_scalar(m_, s_, 1.0 / C, ALU.mult),
                 reads=[f"L4_s{tt}"], writes=[f"L4_m{tt}"])
            p.op("dve", lambda e, m_=m_, mq=mq: e.tensor_mul(mq, m_, m_),
                 reads=[f"L4_m{tt}"], writes=[f"L4_mq{tt}"])
            p.op("dve", lambda e, ss=ss, mq=mq, v_=v_: e.tensor_scalar(
                v_, ss, 1.0 / C, mq, ALU.mult, ALU.subtract),
                reads=[f"L4_ss{tt}", f"L4_mq{tt}"], writes=[f"L4_v{tt}"])

        def p6_s3(tt):
            v_ = t["L4_v"][:, tt:tt + 1]
            lv = t["L4_lv"][:, tt:tt + 1]
            a_ = t["L4_a"][:, tt:tt + 1]
            p.op("act", lambda e, v_=v_, lv=lv: e.activation(lv, v_, AF.Ln, bias=self.epsc[:]),
                 reads=[f"L4_v{tt}", "epsc"], writes=[f"L4_lv{tt}"])
            p.op("act", lambda e, lv=lv, a_=a_: e.activation(a_, lv, AF.Exp, scale=-0.5),
                 reads=[f"L4_lv{tt}"], writes=[f"L4_a{tt}"])

        def p6_s4(tt):
            xb_, xk_ = p6buf[tt % 10]
            m_ = t["L4_m"][:, tt:tt + 1]
            a4 = t["L4_a"][:, tt:tt + 1]
            b4 = t["L4_b"][:, tt:tt + 1]
            f_ap = ffn_acc[:, C * tt:C * tt + C]
            fkeys = [f"acc{tt}_{jo}" for jo in range(4)]
            p.op("dve", lambda e, m_=m_, a4=a4, b4=b4: e.tensor_scalar(
                b4, m_, a4, -1.0, ALU.mult, ALU.mult),
                reads=[f"L4_m{tt}", f"L4_a{tt}"], writes=[f"L4_b{tt}"])
            p.op("act", lambda e, xb_=xb_, b4=b4: e.activation(
                xb_[:], xb_[:], AF.Identity, scale=1.0, bias=b4),
                reads=[xk_, f"L4_b{tt}"], writes=[xk_], tag=f"yb{tt}")
            p.op("dve", lambda e, xb_=xb_, a4=a4, f_ap=f_ap: e.scalar_tensor_tensor(
                xb_[:], f_ap, a4, xb_[:], ALU.mult, ALU.add),
                reads=fkeys + [f"L4_a{tt}", xk_], writes=[xk_], tag=f"ya{tt}")
            p.op("qsp", lambda e, tt=tt, xb_=xb_: e.dma_start(
                out=self.y[128 * tt:128 * tt + 128, :], in_=xb_[:]),
                reads=[xk_], tag=f"sty{tt}")

        for i in range(NT_OWN + 3):
            if i < NT_OWN:
                p6_s1(i)
            if 0 <= i - 1 < NT_OWN:
                p6_s2(i - 1)
            if 0 <= i - 2 < NT_OWN:
                p6_s3(i - 2)
            if 0 <= i - 3 < NT_OWN:
                p6_s4(i - 3)

        self.p.finalize()
        return self.p.final_dma_waits("sync", "qsp")


DEBUG_TAPS = []


def build_program():
    nc = bass.Bass("TRN2", target_bir_lowering=False)
    b = Builder(nc)
    final_waits = b.build()
    for name, shape, dt_, fn in DEBUG_TAPS:
        dto = nc.dram_tensor(f"dbg_{name}", list(shape), dt_, kind="ExternalOutput")
        ap, keys = fn(b)
        b.p.op("qsp", lambda e, dto=dto, ap=ap: e.dma_start(out=dto[:], in_=ap),
               reads=keys, tag=f"dbg_{name}")
    if DEBUG_TAPS:
        b.p.finalize()
        final_waits = b.p.final_dma_waits("sync", "qsp")
    b.p.emit(nc, final_waits=final_waits, verbose=True)
    return nc


def _prep_inputs(inputs):
    x = np.asarray(inputs["L0_attn_x"], np.float32)
    idx = np.arange(3 * C).reshape(NH, 3, HD)
    order = np.concatenate([idx[:, 0].ravel(), idx[:, 1].ravel(), idx[:, 2].ravel()])
    wqT = np.ascontiguousarray(np.asarray(inputs["L0_attn_W_qkv"], np.float32)[order].T)
    woT = np.ascontiguousarray(np.asarray(inputs["L0_attn_W_out"], np.float32).T)
    bo = np.asarray(inputs["L0_attn_b_out"], np.float32).reshape(1, C)
    gwT = np.ascontiguousarray(np.asarray(inputs["L0_ffn_gate_w"], np.float32).T)
    wsgT = np.ascontiguousarray(np.asarray(inputs["L0_ffn_shared_gate"], np.float32).T)
    wsuT = np.ascontiguousarray(np.asarray(inputs["L0_ffn_shared_up"], np.float32).T)
    wsdT = np.ascontiguousarray(np.asarray(inputs["L0_ffn_shared_down"], np.float32).T)
    wrgT = np.ascontiguousarray(np.asarray(inputs["L0_ffn_r_gate"], np.float32).transpose(0, 2, 1))
    wruT = np.ascontiguousarray(np.asarray(inputs["L0_ffn_r_up"], np.float32).transpose(0, 2, 1))
    wrdT = np.ascontiguousarray(np.asarray(inputs["L0_ffn_r_down"], np.float32).transpose(0, 2, 1))
    shared = dict(wqT=wqT, woT=woT, bo=bo, gwT=gwT, wsgT=wsgT, wsuT=wsuT,
                  wsdT=wsdT, wrgT=wrgT, wruT=wruT, wrdT=wrdT)
    in_maps = []
    for core in range(8):
        bb, h = core // 2, core % 2
        own = x[bb, 2048 * h:2048 * h + 2048]
        oth = x[bb, 2048 * (1 - h):2048 * (1 - h) + 2048]
        xc = np.ascontiguousarray(np.concatenate([own, oth], axis=0))
        m = dict(shared)
        m["x_in"] = xc
        in_maps.append(m)
    return in_maps


def kernel(**inputs):
    in_maps = _prep_inputs(inputs)
    nc = build_program()
    res = run_bass_kernel_spmd(nc, in_maps, core_ids=list(range(8)))
    out = np.empty((B, S, C), np.float32)
    for core in range(8):
        bb, h = core // 2, core % 2
        out[bb, 2048 * h:2048 * h + 2048] = res.results[core]["y"]
    return out


# version 37
# speedup vs baseline: 1.0162x; 1.0162x over previous
import contextlib
import numpy as np
import concourse.bass as bass
import concourse.mybir as mybir
from concourse.bass_utils import run_bass_kernel_spmd
from concourse.bass_core_rust import IndirectOffsetOnAxis

F32 = mybir.dt.float32
BF16 = mybir.dt.bfloat16
U32 = mybir.dt.uint32
U8 = mybir.dt.uint8
AF = mybir.ActivationFunctionType
ALU = mybir.AluOpType
AX = mybir.AxisListType

CH_INFO = {
    "pe":    ("tensor", 1),
    "act":   ("scalar", 1),
    "dve":   ("vector", 1),
    "pool":  ("gpsimd", 1),
    "sp":    ("sync", 1),
    "qsp":   ("sync", 16),
    "qact":  ("scalar", 16),
    "qpool": ("gpsimd", 16),
}
COMPUTE_CH = ("pe", "act", "dve", "pool", "sp")
ENGINE_OF = {c: CH_INFO[c][0] for c in CH_INFO}


class Op:
    __slots__ = ("ch", "eng", "emit", "reads", "writes", "gidx", "deps",
                 "need_inc", "cum", "waits", "guar", "tag", "sem", "reuse_wait")

    def __init__(self, ch, emit, reads, writes, gidx, tag):
        self.ch = ch
        self.eng = ENGINE_OF[ch]
        self.emit = emit
        self.reads = reads
        self.writes = writes
        self.gidx = gidx
        self.deps = []
        self.need_inc = False
        self.cum = None
        self.waits = None
        self.guar = None
        self.tag = tag


class Prog:
    def __init__(self):
        self.ops = []
        self.last_writer = {}
        self.readers = {}
        self.by_engine = {}

    def op(self, ch, emit, reads=(), writes=(), tag="", relax_raw=False):
        o = Op(ch, emit, tuple(reads), tuple(writes), len(self.ops), tag)
        deps = {}
        for r in o.reads:
            w = self.last_writer.get(r)
            if w is not None:
                deps[w.gidx] = (w, "raw")
        for w_ in o.writes:
            lw = self.last_writer.get(w_)
            if lw is not None and lw.gidx not in deps:
                deps[lw.gidx] = (lw, "waw")
            for rd in self.readers.get(w_, ()):
                if rd.gidx not in deps:
                    deps[rd.gidx] = (rd, "war")
        for r in o.reads:
            self.readers.setdefault(r, []).append(o)
        for w_ in o.writes:
            self.last_writer[w_] = o
            self.readers[w_] = []
        kept = []
        for d, kind in deps.values():
            if d is o:
                continue
            if d.ch == o.ch and d.ch in COMPUTE_CH:
                if kind == "waw" or (kind == "raw" and relax_raw):
                    continue
            elif d.ch == o.ch:
                pass
            kept.append(d)
        o.deps = kept
        self.ops.append(o)
        self.by_engine.setdefault(o.eng, []).append(o)
        return o

    SEM_POOL = {"qsp": 6, "qact": 4, "qpool": 12}

    def finalize(self):
        for o in self.ops:
            best = {}
            extra = []
            for d in o.deps:
                if d.ch in COMPUTE_CH:
                    if d.ch not in best or d.gidx > best[d.ch].gidx:
                        best[d.ch] = d
                else:
                    extra.append(d)
            o.deps = list(best.values()) + extra
        for o in self.ops:
            if o.ch not in COMPUTE_CH:
                o.need_inc = True
            for d in o.deps:
                d.need_inc = True
        cum = {}
        pool_state = {}
        for o in self.ops:
            if o.ch in COMPUTE_CH:
                o.sem = o.ch
                if o.need_inc:
                    cum[o.sem] = cum.get(o.sem, 0) + 1
                o.cum = cum.get(o.sem, 0)
                o.reuse_wait = None
            else:
                st = pool_state.setdefault(o.ch, [0, {}])
                sid = st[0] % self.SEM_POOL[o.ch]
                st[0] += 1
                o.sem = f"{o.ch}_{sid}"
                prev = st[1].get(sid)
                o.reuse_wait = (o.sem, prev.cum, prev) if prev is not None else None
                cum[o.sem] = cum.get(o.sem, 0) + 16
                o.cum = cum[o.sem]
                st[1][sid] = o
        prev_on_channel = {}
        for o in self.ops:
            g = dict(prev_on_channel[o.ch].guar) if o.ch in prev_on_channel and o.ch in COMPUTE_CH else {}
            if o.ch not in COMPUTE_CH:
                pass
            g[o.sem] = max(g.get(o.sem, 0), o.cum)
            o.waits = []
            if o.reuse_wait is not None:
                s_, v_, d_ = o.reuse_wait
                o.waits.append((s_, v_, d_))
            for d in o.deps:
                o.waits.append((d.sem, d.cum, d))
            for s_, v_, d_ in o.waits:
                if d_ is not None:
                    for c, v in d_.guar.items():
                        if v > g.get(c, 0):
                            g[c] = v
            o.guar = g
            prev_on_channel[o.ch] = o
        known = {}
        for o in self.ops:
            k = known.setdefault(o.eng, {})
            keep = []
            for s_, v, d in sorted(o.waits, key=lambda t: -t[1]):
                if k.get(s_, 0) >= v:
                    continue
                keep.append((s_, v))
                k[s_] = v
                if d is not None:
                    for cc, vv in d.guar.items():
                        if vv > k.get(cc, 0):
                            k[cc] = vv
            o.waits = keep
        self.final_cum = cum
        return cum

    def final_dma_waits(self, engine, ch):
        return [(engine, s, v) for s, v in self.final_cum.items()
                if s.startswith(ch + "_")]

    def emit(self, nc, final_waits=None, verbose=False):
        cum = self.finalize()
        if verbose:
            n_w = sum(len(o.waits) for o in self.ops)
            n_i = sum(1 for o in self.ops if o.need_inc)
            tot = {c: sum(v for s, v in cum.items() if s == c or s.startswith(c + "_"))
                   for c in CH_INFO}
            print(f"fw: {len(self.ops)} ops, {n_i} incs, {n_w} waits, cum={tot}")
        with contextlib.ExitStack() as stack:
            sems = {}
            names = set(cum) | {fw[1] for fw in (final_waits or [])}
            for s_ in sorted(names):
                sems[s_] = stack.enter_context(nc.semaphore(name=f"sem_{s_}"))
            with nc.Block() as block:
                for eng_name in ("sync", "tensor", "scalar", "vector", "gpsimd"):
                    ops = self.by_engine.get(eng_name, [])
                    fwaits = [fw for fw in (final_waits or []) if fw[0] == eng_name]
                    if not ops and not fwaits:
                        continue

                    def body(eng, ops=ops, fwaits=fwaits):
                        for o in ops:
                            for s_, v in o.waits:
                                eng.wait_ge(sems[s_], v)
                            inst = o.emit(eng)
                            if o.need_inc:
                                inst.then_inc(sems[o.sem],
                                              16 if o.ch not in COMPUTE_CH else 1)
                        for _, s_, v in fwaits:
                            eng.wait_ge(sems[s_], v)

                    getattr(block, eng_name)(body)
        return cum


B, S, C = 4, 4096, 1024
NH, HD = 16, 64
FF, NE = 2048, 8
TOK = 2048
NT_OWN, NT_ALL = 16, 32
NDC = 8
NFC = 16
EPS = 1e-5
QSCALE = 1.0 / 32.0
GROUP_HEADS = 4
NGROUP = NH // GROUP_HEADS
GC = GROUP_HEADS * HD
VW = GROUP_HEADS * (HD + 1)

SBUF_END = 229344

_L = {}
_off = [18560]


def _place(name, nbytes):
    _L[name] = _off[0]
    _off[0] += (nbytes + 63) & ~63
    return _L[name]


for _n, _b in [("ident", 512), ("ones_bf", 256), ("gw_sb", 256), ("ones8", 32),
               ("scores", 64), ("rmax_all", 64), ("alpha", 64), ("beta", 64),
               ("logits_all", 512), ("mask_all", 512)]:
    _place(_n, _b)
for _pre, _ncols in [("L1", 32), ("L2", 16), ("L3", 16), ("L4", 16)]:
    for _nm in ("s", "ss", "m", "mq", "v", "lv", "a", "b"):
        _place(f"{_pre}_{_nm}", 4 * _ncols)
for _n, _b in [("eqb", 32), ("prefb", 32), ("epsc", 64), ("scr", 4096),
               ("x1b0", 4096), ("x1b1", 4096), ("ln3b", 4096),
               ("bo_bf", 2048), ("b_bcast", 4096), ("tmpf0", 512), ("tmpf1", 512)]:
    _place(_n, _b)
_PEND = (_off[0] + 63) & ~63

_off[0] = _PEND
for _dc in range(NDC):
    _place(f"O_fm{_dc}", 4096)
for _dc in range(NDC):
    _place(f"ln1_fm{_dc}", 8192)
for _n, _b in [("xt0", 4096), ("xt1", 4096), ("wq_sb", 4096), ("wk_sb", 4096),
               ("wv_sb", 4096), ("Q_sb0", 4096), ("Q_sb1", 4096),
               ("K_sb0", 8192), ("K_sb1", 8192), ("V_sb", 16640),
               ("A01_0", 2048), ("A01_1", 2048), ("A23_0", 2048), ("A23_1", 2048),
               ("srow", 2048), ("trow", 2048), ("rse_bf", 1024),
               ("rse_t0", 2048), ("rse_t1", 2048)]:
    _place(_n, _b)
assert _off[0] <= SBUF_END, f"region A overflow: {_off[0]}"
_L["p0q0"] = _L["Q_sb0"]
_L["p0q1"] = _L["Q_sb1"]
_L["p0k0a"] = _L["K_sb0"]
_L["p0k0b"] = _L["K_sb0"] + 4096

_off[0] = _L["ln1_fm0"]
for _dc in range(NDC):
    _place(f"fm{_dc}", 4096)
for _n, _b in [("wo_sb", 16384), ("xo0", 4096), ("xo1", 4096),
               ("tb1_0", 4096), ("tb1_1", 4096), ("aob0", 4096), ("aob1", 4096),
               ("zb0", 4096), ("zb1", 4096), ("ln3c0", 4096), ("ln3c1", 4096)]:
    _place(_n, _b)
assert _off[0] <= SBUF_END

_off[0] = _L["O_fm0"]
for _n, _b in [("wd_b0", 8192), ("wd_b1", 8192), ("wgu_b0", 4096), ("wgu_b1", 4096),
               ("silb0", 2048), ("silb1", 2048)]:
    _place(_n, _b)
assert _off[0] <= _L["ln1_fm0"], "region C head overflows into fm"
_off[0] = _L["wo_sb"]
_place("ffn_acc", 65536)
_place("pe_half", 32768)
for _dc in range(NDC):
    _place(f"xef{_dc}", 1024)
for _n, _b in [("ident_bf", 256), ("gbuf0", 2048), ("gbuf1", 2048), ("gbuf2", 2048),
               ("Ltri", 256), ("mask_em", 512), ("c_all", 512), ("ones16", 64),
               ("T8b", 64), ("Pexcl", 64), ("slotg", 64), ("slotu", 64),
               ("selm", 64), ("selmu", 64), ("t1b", 64), ("tok_iota", 64),
               ("pref_sb", 128), ("idx_sb", 128)]:
    _place(_n, _b)
assert _off[0] <= SBUF_END, f"region C overflow: {_off[0]}"
CAP = 384
CAPK = CAP // 128
NSLOT = NE * CAPK


class Builder:
    def __init__(self, nc):
        self.nc = nc
        self.p = Prog()
        self.t = {}

        d = nc.dram_tensor
        self.x_in = d("x_in", [S, C], F32, kind="ExternalInput")
        self.wqT = d("wqT", [C, 3 * C], F32, kind="ExternalInput")
        self.woT = d("woT", [C, C], F32, kind="ExternalInput")
        self.bo = d("bo", [1, C], F32, kind="ExternalInput")
        self.gwT = d("gwT", [C, NE], F32, kind="ExternalInput")
        self.wsgT = d("wsgT", [C, FF], F32, kind="ExternalInput")
        self.wsuT = d("wsuT", [C, FF], F32, kind="ExternalInput")
        self.wsdT = d("wsdT", [FF, C], F32, kind="ExternalInput")
        self.wrgT = d("wrgT", [NE, C, FF], F32, kind="ExternalInput")
        self.wruT = d("wruT", [NE, C, FF], F32, kind="ExternalInput")
        self.wrdT = d("wrdT", [NE, FF, C], F32, kind="ExternalInput")
        self.y = d("y", [TOK, C], F32, kind="ExternalOutput")
        self.x1sp = d("x1sp", [TOK, C], F32, kind="Internal")
        self.xssp = d("xssp", [TOK, C], BF16, kind="Internal")
        self.rout_sp = d("rout_sp", [TOK + NE * CAP, C], F32, kind="Internal")
        self.wsgu_bf = d("wsgu_bf", [C, NFC, 256], BF16, kind="Internal")
        self.wsd_bf = d("wsd_bf", [FF, C], BF16, kind="Internal")
        self.wrgu_bf = d("wrgu_bf", [NE, C, NFC, 256], BF16, kind="Internal")
        self.wrd_bf = d("wrd_bf", [NE, FF, C], BF16, kind="Internal")
        self.idx_dram = d("idx_dram", [NE * CAP, 1], mybir.dt.uint32, kind="Internal")

    def at(self, name, shape, dt=F32):
        h = self.nc.alloc_sbuf_tensor_at(name, shape, dt, offset=_L[name])
        self.t[name] = h
        return h

    def psum_setup(self):
        self._psg = self.nc.psum_tensor("ps_big", [128, 2048], F32)
        self._psa = self.nc.psum_tensor("ps_av", [128, 2048], F32)
        self.ps_big = self._psg.__enter__()
        self.ps_av = self._psa.__enter__()

    def pb(self, c0, c1, p0=0, p1=128):
        keys = tuple(f"pb{b}" for b in range(c0 // 512, (c1 + 511) // 512))
        return self.ps_big[p0:p1, c0:c1], keys

    def pa(self, c0, c1, p0=0, p1=128):
        keys = tuple(f"pa{b}" for b in range(c0 // 512, (c1 + 511) // 512))
        return self.ps_av[p0:p1, c0:c1], keys

    def ln_stats(self, x_ap, x_keys, pre, col, scr, scr_key):
        p, t = self.p, self.t
        s_ = t[pre + "_s"][:, col:col + 1]
        ss = t[pre + "_ss"][:, col:col + 1]
        m_ = t[pre + "_m"][:, col:col + 1]
        mq = t[pre + "_mq"][:, col:col + 1]
        v_ = t[pre + "_v"][:, col:col + 1]
        lv = t[pre + "_lv"][:, col:col + 1]
        a_ = t[pre + "_a"][:, col:col + 1]
        b_ = t[pre + "_b"][:, col:col + 1]
        ks = lambda n: f"{pre}_{n}{col}"
        p.op("dve", lambda e: e.reduce_sum(s_, x_ap, axis=AX.X),
             reads=x_keys, writes=[ks("s")], tag=f"{pre}{col}_rs")
        p.op("act", lambda e: e.activation(scr, x_ap, AF.Square, accum_out=ss),
             reads=list(x_keys) + [scr_key], writes=[ks("ss"), scr_key],
             tag=f"{pre}{col}_sq")
        p.op("dve", lambda e: e.tensor_single_scalar(m_, s_, 1.0 / C, ALU.mult),
             reads=[ks("s")], writes=[ks("m")])
        p.op("dve", lambda e: e.tensor_mul(mq, m_, m_),
             reads=[ks("m")], writes=[ks("mq")])
        p.op("dve", lambda e: e.tensor_scalar(v_, ss, 1.0 / C, mq, ALU.mult, ALU.subtract),
             reads=[ks("ss"), ks("mq")], writes=[ks("v")])
        p.op("act", lambda e: e.activation(lv, v_, AF.Ln, bias=self.epsc[:]),
             reads=[ks("v"), "epsc"], writes=[ks("lv")])
        p.op("act", lambda e: e.activation(a_, lv, AF.Exp, scale=-0.5),
             reads=[ks("lv")], writes=[ks("a")])
        p.op("dve", lambda e: e.tensor_scalar(b_, m_, a_, -1.0, ALU.mult, ALU.mult),
             reads=[ks("m"), ks("a")], writes=[ks("b")])
        return a_, b_, (ks("a"), ks("b"))

    def ln_apply(self, out_ap, out_keys, x_ap, x_keys, a_, b_, ab_keys, tag=""):
        self.p.op("dve",
                  lambda e: e.tensor_scalar(out_ap, x_ap, a_, b_, ALU.mult, ALU.add),
                  reads=tuple(x_keys) + tuple(ab_keys), writes=out_keys, tag=tag)

    def build(self):
        nc, p = self.nc, self.p
        t = self.t
        self.psum_setup()

        ident = self.at("ident", [128, 128], F32)
        ones_bf = self.at("ones_bf", [1, 128], BF16)
        gw_sb = self.at("gw_sb", [128, NDC * NE], F32)
        ones8 = self.at("ones8", [128, NE], F32)
        scores = self.at("scores", [128, NT_OWN], F32)
        rmax_all = self.at("rmax_all", [128, NT_OWN], F32)
        alpha = self.at("alpha", [128, NT_OWN], F32)
        beta = self.at("beta", [128, NT_OWN], F32)
        logits_all = self.at("logits_all", [128, NT_OWN * NE], F32)
        mask_all = self.at("mask_all", [128, NT_OWN * NE], F32)
        for pre, ncols in [("L1", NT_ALL), ("L2", NT_OWN), ("L3", NT_OWN), ("L4", NT_OWN)]:
            for nm in ("s", "ss", "m", "mq", "v", "lv", "a", "b"):
                self.at(f"{pre}_{nm}", [128, ncols], F32)
        eqb = self.at("eqb", [128, NE], F32)
        prefb = self.at("prefb", [128, NE], F32)
        epsc = self.at("epsc", [128, 1], F32)
        self.epsc = epsc
        p.op("pool", lambda e: e.memset(epsc[:], EPS), writes=["epsc"])
        scr = self.at("scr", [128, C], F32)
        x1b = [self.at(f"x1b{i}", [128, C], F32) for i in range(2)]
        ln3b = self.at("ln3b", [128, C], F32)
        bo_bf = self.at("bo_bf", [1, C], BF16)
        b_bcast = self.at("b_bcast", [128, C], F32)
        tmpf = [self.at(f"tmpf{i}", [128, 128], F32) for i in range(2)]

        self.rbnd = {}

        def _regs(e):
            r1 = e.alloc_register("bndg")
            e.reg_mov(r1, TOK - 1)
            r2 = e.alloc_register("bnds")
            inst = e.reg_mov(r2, NE * CAP - 1)
            self.rbnd["g"] = r1
            self.rbnd["s"] = r2
            return inst

        p.op("pool", _regs)
        p.op("pool", lambda e: e.memset(ident[:], 0.0), writes=["ident"])
        p.op("pool", lambda e: e.affine_select(out=ident[:], in_=ident[:],
                                               compare_op=ALU.not_equal, fill=1.0,
                                               base=0, pattern=[[-1, 128]],
                                               channel_multiplier=1),
             reads=["ident"], writes=["ident"])
        p.op("pool", lambda e: e.memset(ones_bf[:], 1.0), writes=["ones"])
        p.op("pool", lambda e: e.memset(ones8[:], 1.0), writes=["ones8"])
        p.op("qsp", lambda e: e.dma_start(
            out=gw_sb[:].rearrange("p (dc e) -> p dc e", dc=NDC),
            in_=self.gwT[:].rearrange("(dc p) e -> p dc e", p=128)),
            writes=["gw"])
        p.op("qpool", lambda e: e.dma_start(out=bo_bf[:], in_=self.bo[:]),
             writes=["bo_bf"])
        for j in range(2):
            ap, keys = self.pb(512 * j, 512 * j + 512)
            p.op("pe", lambda e, ap=ap, j=j: e.matmul(
                ap, ones_bf[:], bo_bf[:, 512 * j:512 * j + 512], start=True, stop=True),
                reads=["ones", "bo_bf"], writes=keys)
        ap, keys = self.pb(0, 1024)
        p.op("dve", lambda e, ap=ap: e.tensor_copy(b_bcast[:], ap),
             reads=keys, writes=["b_bcast"])

        ln1_fm = [self.at(f"ln1_fm{dc}", [128, S], BF16) for dc in range(NDC)]
        xt = [self.at(f"xt{i}", [128, C], F32) for i in range(2)]
        p0b = [self.at(n, [128, C], F32)
               for n in ("p0q0", "p0q1", "p0k0a", "p0k0b")]
        xpool = [(xt[0], "xt0"), (xt[1], "xt1"), (x1b[0], "x1b0"), (x1b[1], "x1b1"),
                 (p0b[0], "p0q0"), (p0b[1], "p0q1"),
                 (p0b[2], "p0k0a"), (p0b[3], "p0k0b")]
        def p0_s1(tt):
            xb_, xk_ = xpool[tt % 8]
            p.op("qsp", lambda e, tt=tt, xb_=xb_: e.dma_start(
                out=xb_[:], in_=self.x_in[128 * tt:128 * tt + 128, :]),
                writes=[xk_], tag=f"ldx{tt}")
            s_ = t["L1_s"][:, tt:tt + 1]
            ss = t["L1_ss"][:, tt:tt + 1]
            p.op("dve", lambda e, xb_=xb_, s_=s_: e.reduce_sum(s_, xb_[:], axis=AX.X),
                 reads=[xk_], writes=[f"L1_s{tt}"], tag=f"L1{tt}_rs")
            p.op("act", lambda e, xb_=xb_, ss=ss: e.activation(
                scr[:], xb_[:], AF.Square, accum_out=ss),
                reads=[xk_, "scr"], writes=[f"L1_ss{tt}", "scr"], tag=f"L1{tt}_sq")

        def p0_s2(tt):
            s_ = t["L1_s"][:, tt:tt + 1]
            ss = t["L1_ss"][:, tt:tt + 1]
            m_ = t["L1_m"][:, tt:tt + 1]
            mq = t["L1_mq"][:, tt:tt + 1]
            v_ = t["L1_v"][:, tt:tt + 1]
            p.op("dve", lambda e, s_=s_, m_=m_: e.tensor_single_scalar(m_, s_, 1.0 / C, ALU.mult),
                 reads=[f"L1_s{tt}"], writes=[f"L1_m{tt}"])
            p.op("dve", lambda e, m_=m_, mq=mq: e.tensor_mul(mq, m_, m_),
                 reads=[f"L1_m{tt}"], writes=[f"L1_mq{tt}"])
            p.op("dve", lambda e, ss=ss, mq=mq, v_=v_: e.tensor_scalar(
                v_, ss, 1.0 / C, mq, ALU.mult, ALU.subtract),
                reads=[f"L1_ss{tt}", f"L1_mq{tt}"], writes=[f"L1_v{tt}"])

        def p0_s3(tt):
            v_ = t["L1_v"][:, tt:tt + 1]
            lv = t["L1_lv"][:, tt:tt + 1]
            a_ = t["L1_a"][:, tt:tt + 1]
            p.op("act", lambda e, v_=v_, lv=lv: e.activation(lv, v_, AF.Ln, bias=self.epsc[:]),
                 reads=[f"L1_v{tt}", "epsc"], writes=[f"L1_lv{tt}"])
            p.op("act", lambda e, lv=lv, a_=a_: e.activation(a_, lv, AF.Exp, scale=-0.5),
                 reads=[f"L1_lv{tt}"], writes=[f"L1_a{tt}"])

        def p0_s4(tt):
            xb_, xk_ = xpool[tt % 8]
            m_ = t["L1_m"][:, tt:tt + 1]
            a_ = t["L1_a"][:, tt:tt + 1]
            b_ = t["L1_b"][:, tt:tt + 1]
            p.op("dve", lambda e, m_=m_, a_=a_, b_=b_: e.tensor_scalar(
                b_, m_, a_, -1.0, ALU.mult, ALU.mult),
                reads=[f"L1_m{tt}", f"L1_a{tt}"], writes=[f"L1_b{tt}"])
            p.op("dve", lambda e, xb_=xb_, a_=a_, b_=b_: e.tensor_scalar(
                xb_[:], xb_[:], a_, b_, ALU.mult, ALU.add),
                reads=[xk_, f"L1_a{tt}", f"L1_b{tt}"], writes=[xk_],
                tag=f"ln1app{tt}")
            for dc in range(NDC):
                bank = (tt * NDC + dc) % 4
                tp_ap, tp_keys = self.pa(512 * bank, 512 * bank + 128)
                p.op("pe", lambda e, dc=dc, xb_=xb_, tp_ap=tp_ap: e.transpose(
                    tp_ap, xb_[:, 128 * dc:128 * dc + 128], ident[:]),
                    reads=[xk_, "ident"], writes=tp_keys, tag=f"tr{tt}_{dc}")
                ech = "dve" if dc < 3 else "act"
                def _ev(e, dc=dc, tt=tt, tp_ap=tp_ap):
                    if hasattr(e, "tensor_copy"):
                        return e.tensor_copy(ln1_fm[dc][:, 128 * tt:128 * tt + 128], tp_ap)
                    return e.activation(ln1_fm[dc][:, 128 * tt:128 * tt + 128], tp_ap, AF.Copy)
                p.op(ech, _ev, reads=tp_keys, writes=[f"ln1fm{dc}_{tt}"],
                     tag=f"trev{tt}_{dc}")

        for i in range(NT_ALL + 3):
            if i < NT_ALL:
                p0_s1(i)
            if 0 <= i - 1 < NT_ALL:
                p0_s2(i - 1)
            if 0 <= i - 2 < NT_ALL:
                p0_s3(i - 2)
            if 0 <= i - 3 < NT_ALL:
                p0_s4(i - 3)

        O_fm = [self.at(f"O_fm{dc}", [128, TOK], BF16) for dc in range(NDC)]
        wq_sb = self.at("wq_sb", [128, NDC * GC], BF16)
        wk_sb = self.at("wk_sb", [128, NDC * GC], BF16)
        wv_sb = self.at("wv_sb", [128, NDC * GC], BF16)
        Q_sb = [self.at(f"Q_sb{m}", [128, TOK], BF16) for m in range(2)]
        K_sb = [self.at(f"K_sb{m}", [128, S], BF16) for m in range(2)]
        V_sb = self.at("V_sb", [128, NT_ALL * VW], BF16)
        A01 = [self.at(f"A01_{i}", [128, 1024], BF16) for i in range(2)]
        A23 = [self.at(f"A23_{i}", [128, 1024], BF16) for i in range(2)]
        srow = self.at("srow", [1, 512], F32)
        trow = self.at("trow", [1, 512], F32)
        rse_bf = self.at("rse_bf", [1, 512], BF16)
        trow1 = self.nc.alloc_sbuf_tensor_at("trow1", [1, 512], F32,
                                             offset=_L["srow"])
        rse_bf1 = self.nc.alloc_sbuf_tensor_at("rse_bf1", [1, 512], BF16,
                                               offset=_L["A23_1"])
        trow_h = [(trow, "trow"), (trow1, "srow")]
        rse_h = [(rse_bf, "rsebf"), (rse_bf1, "rsebf1")]
        rse_t = [self.at(f"rse_t{i}", [64, 512], F32) for i in range(2)]

        p.op("pool", lambda e: e.memset(V_sb[:], 0.0),
             writes=[f"V{tt}" for tt in range(NT_ALL)], tag="vzero")
        for tt in range(NT_OWN):
            p.op("qpool", lambda e, tt=tt: e.dma_start(
                out=self.rout_sp[128 * tt:128 * tt + 128, :], in_=V_sb[:, 0:C]),
                reads=["V0", "V1", "V2", "V3"], writes=[f"rz{tt}"], tag=f"rz{tt}")
        p.op("pool", lambda e: e.memset(V_sb[:], 1.0),
             writes=[f"V{tt}" for tt in range(NT_ALL)], tag="vones")

        def emit_precast(g):
            if g == 0:
                for half, srcT in ((0, self.wsgT), (1, self.wsuT)):
                    for dcc in range(0, C, 256):
                        p.op("qpool", lambda e, srcT=srcT, half=half, dcc=dcc: e.dma_start(
                            out=self.wsgu_bf[dcc:dcc + 256, :, 128 * half:128 * half + 128],
                            in_=srcT[dcc:dcc + 256, :].rearrange("c (f k) -> c f k", f=NFC)),
                            writes=["wbfsgu"], tag=f"pcs{half}_{dcc}")
                p.op("qpool", lambda e: e.dma_start(
                    out=self.wsd_bf[:], in_=self.wsdT[:]),
                    writes=["wbfsd"], tag="pcsd")
            for ei in (2 * g, 2 * g + 1):
                for half, srcT in ((0, self.wrgT), (1, self.wruT)):
                    for dcc in range(0, C, 256):
                        p.op("qpool", lambda e, srcT=srcT, half=half, ei=ei, dcc=dcc: e.dma_start(
                            out=self.wrgu_bf[ei][dcc:dcc + 256, :, 128 * half:128 * half + 128],
                            in_=srcT[ei][dcc:dcc + 256, :].rearrange("c (f k) -> c f k", f=NFC)),
                            writes=[f"wbfgu{ei}"], tag=f"pcgu{ei}_{half}_{dcc}")
                p.op("qpool", lambda e, ei=ei: e.dma_start(
                    out=self.wrd_bf[ei], in_=self.wrdT[ei]),
                    writes=[f"wbfrd{ei}"], tag=f"pcrd{ei}")

        for g in range(NGROUP):
            for nm, sb_, base in (("q", wq_sb, 0), ("k", wk_sb, C), ("v", wv_sb, 2 * C)):
                p.op("qpool", lambda e, sb_=sb_, base=base, g=g: e.dma_start(
                    out=sb_[:].rearrange("p (dc f) -> p dc f", dc=NDC),
                    in_=self.wqT[:, base + GC * g: base + GC * g + GC]
                        .rearrange("(dc p) f -> p dc f", p=128)),
                    writes=[f"w{nm}"], tag=f"ldw{nm}{g}")
            emit_precast(g)
            for nm, sb_w, dst, ntb in (("q", wq_sb, Q_sb, 4), ("k", wk_sb, K_sb, 8)):
                for m2 in range(2):
                    for tb in range(ntb):
                        qp, qk = self.pb(512 * (tb % 4), 512 * (tb % 4) + 512)
                        for dc in range(NDC):
                            p.op("pe", lambda e, m2=m2, tb=tb, dc=dc, qp=qp, sb_w=sb_w: e.matmul(
                                qp, sb_w[:, GC * dc + 128 * m2: GC * dc + 128 * m2 + 128],
                                ln1_fm[dc][:, 512 * tb:512 * tb + 512],
                                start=(dc == 0), stop=(dc == NDC - 1)),
                                reads=[f"w{nm}"] + [f"ln1fm{dc}_{i}" for i in range(4 * tb, 4 * tb + 4)],
                                writes=qk, tag=f"{nm}mm{g}_{m2}_{tb}")
                        if nm == "q":
                            guard = [f"p0q{m2}"] if g == 0 else []
                            p.op("dve", lambda e, m2=m2, tb=tb, qp=qp, dst=dst: e.tensor_single_scalar(
                                dst[m2][:, 512 * tb:512 * tb + 512], qp, QSCALE, ALU.mult),
                                reads=qk, writes=[f"Q{m2}"] + guard, tag=f"qev{g}_{m2}_{tb}")
                        else:
                            guard = ([f"p0k0{'a' if tb < 4 else 'b'}"]
                                     if (g == 0 and m2 == 0) else [])
                            p.op("dve", lambda e, m2=m2, tb=tb, qp=qp, dst=dst: e.tensor_copy(
                                dst[m2][:, 512 * tb:512 * tb + 512], qp),
                                reads=qk, writes=[f"K{m2}"] + guard, tag=f"kev{g}_{m2}_{tb}")
            for tt in range(NT_ALL):
                vp, vk = self.pa(1024 + 512 * (tt % 2), 1024 + 512 * (tt % 2) + GC)
                for dc in range(NDC):
                    p.op("pe", lambda e, tt=tt, dc=dc, vp=vp: e.matmul(
                        vp, ln1_fm[dc][:, 128 * tt:128 * tt + 128],
                        wv_sb[:, GC * dc:GC * dc + GC],
                        start=(dc == 0), stop=(dc == NDC - 1)),
                        reads=["wv", f"ln1fm{dc}_{tt}"], writes=vk, tag=f"vmm{g}_{tt}")
                p.op("dve", lambda e, tt=tt, vp=vp: e.tensor_copy(
                    V_sb[:, VW * tt:VW * tt + VW]
                        .rearrange("p (j s) -> p j s", j=GROUP_HEADS, s=HD + 1)[:, :, 0:HD],
                    vp.rearrange("p (j s) -> p j s", j=GROUP_HEADS, s=HD)),
                    reads=vk, writes=[f"V{tt}"], tag=f"vev{g}_{tt}")

            def emit_S(kc, qb, m2, hpairbase):
                kpar = kc % 2
                for hb in range(2):
                    sp_ap, sp_k = self.pb(1024 * kpar + 512 * hb,
                                          1024 * kpar + 512 * hb + 512)
                    p.op("pe", lambda e, sp_ap=sp_ap, m2=m2, hb=hb, kc=kc, qb=qb: e.matmul(
                        sp_ap, K_sb[m2][64 * hb:64 * hb + 64, 128 * kc:128 * kc + 128],
                        Q_sb[m2][64 * hb:64 * hb + 64, 512 * qb:512 * qb + 512],
                        start=True, stop=True),
                        reads=[f"K{m2}", f"Q{m2}"], writes=sp_k,
                        tag=f"S{hpairbase}_{hb}_{kc}_{qb}")

            prefilled = [False]
            for m2 in range(2):
                hpairbase = GROUP_HEADS * g + 2 * m2
                dcO = (hpairbase) // 2
                for qb in range(4):
                    qpar = qb % 2
                    if not prefilled[0]:
                        emit_S(0, qb, m2, hpairbase)
                        emit_S(1, qb, m2, hpairbase)
                    prefilled[0] = False
                    Abufs = [A01[0], A01[1], A23[0]]
                    for kc in range(NT_ALL):
                        kpar = kc % 2
                        apar = kc % 3
                        e_ap, e_k = self.pb(1024 * kpar, 1024 * kpar + 1024)
                        Abuf = Abufs[apar]
                        p.op("act", lambda e, e_ap=e_ap, Abuf=Abuf: e.activation(
                            Abuf[:], e_ap, AF.Exp),
                            reads=e_k, writes=[f"A{apar}"],
                            tag=f"E{hpairbase}_{qb}_{kc}")
                        if kc + 2 < NT_ALL:
                            emit_S(kc + 2, qb, m2, hpairbase)
                        for hb in range(2):
                            j = 2 * m2 + hb
                            av_ap, av_k = self.pa(1024 * qpar + 512 * hb,
                                                  1024 * qpar + 512 * hb + 512, 0, 65)
                            p.op("pe", lambda e, av_ap=av_ap, Abuf=Abuf, kc=kc, hb=hb, j=j: e.matmul(
                                av_ap, V_sb[:, VW * kc + (HD + 1) * j: VW * kc + (HD + 1) * j + (HD + 1)],
                                Abuf[:, 512 * hb:512 * hb + 512],
                                start=(kc == 0), stop=(kc == NT_ALL - 1)),
                                reads=[f"V{kc}", f"A{apar}"], writes=av_k,
                                tag=f"AV{hpairbase}_{hb}_{kc}_{qb}")
                    nqb, nm2 = (qb + 1, m2) if qb < 3 else (0, m2 + 1)
                    if nm2 < 2:
                        nhp = GROUP_HEADS * g + 2 * nm2
                        emit_S(0, nqb, nm2, nhp)
                        emit_S(1, nqb, nm2, nhp)
                        prefilled[0] = True
                    for hb in range(2):
                        h = hpairbase + hb
                        epar = hb
                        tr_, trk = trow_h[hb]
                        rs_, rsk = rse_h[hb]
                        av_full, avf_k = self.pa(1024 * qpar + 512 * hb,
                                                 1024 * qpar + 512 * hb + 512)
                        p.op("act", lambda e, av_full=av_full, tr_=tr_: e.activation(
                            tr_[:], av_full[64:65, :], AF.Ln),
                            reads=avf_k, writes=[trk], tag=f"se{h}_{qb}")
                        p.op("act", lambda e, tr_=tr_, rs_=rs_: e.activation(
                            rs_[:], tr_[:], AF.Exp, scale=-1.0),
                            reads=[trk], writes=[rsk])
                        bc_ap, bc_k = self.pa(1024 * (1 - qpar) + 512 * hb,
                                              1024 * (1 - qpar) + 512 * hb + 512, 0, 64)
                        p.op("pe", lambda e, bc_ap=bc_ap, rs_=rs_: e.matmul(
                            bc_ap, ones_bf[:, 0:64], rs_[:], start=True, stop=True),
                            reads=["ones", rsk], writes=bc_k, tag=f"bc{h}_{qb}")
                        p.op("dve", lambda e, bc_ap=bc_ap, epar=epar: e.tensor_copy(
                            rse_t[epar][:], bc_ap),
                            reads=bc_k, writes=[f"rset{epar}"])
                        av_o, avo_k = self.pa(1024 * qpar + 512 * hb,
                                              1024 * qpar + 512 * hb + 512, 0, 64)
                        offO = 64 * (h % 2)
                        p.op("dve", lambda e, av_o=av_o, epar=epar, dcO=dcO, offO=offO, qb=qb: e.tensor_tensor(
                            O_fm[dcO][offO:offO + 64, 512 * qb:512 * qb + 512],
                            av_o, rse_t[epar][:], ALU.mult),
                            reads=avo_k + (f"rset{epar}",), writes=[f"O{dcO}_{qb}"],
                            tag=f"On{h}_{qb}")

        fm_buf = [self.at(f"fm{dc}", [128, TOK], BF16) for dc in range(NDC)]
        wo_sb = self.at("wo_sb", [128, NDC * C], BF16)
        xo = [self.at(f"xo{i}", [128, C], F32) for i in range(2)]
        tb1 = [self.at(f"tb1_{i}", [128, C], F32) for i in range(2)]
        aob = [self.at(f"aob{i}", [128, C], F32) for i in range(2)]
        zb = [self.at(f"zb{i}", [128, C], F32) for i in range(2)]
        ln3c = [self.at(f"ln3c{i}", [128, C], F32) for i in range(2)]
        p.op("qpool", lambda e: e.dma_start(
            out=wo_sb[:].rearrange("p (dc f) -> p dc f", dc=NDC),
            in_=self.woT[:].rearrange("(dc p) f -> p dc f", p=128)),
            writes=["wo"] + [f"ln1fm{dc}_{i}" for dc in (4, 5) for i in range(NT_ALL)],
            tag="ldwo")

        def p4a_mms(tt):
            par = tt % 2
            xo_guard = [f"ln1fm6_{i}" for i in range(NT_ALL)] if tt < 2 else []
            p.op("qsp", lambda e, tt=tt, par=par: e.dma_start(
                out=xo[par][:], in_=self.x_in[128 * tt:128 * tt + 128, :]),
                writes=[f"xo{par}"] + xo_guard, tag=f"ldxo{tt}")
            for jh in range(2):
                for dc in range(NDC):
                    p.op("pe", lambda e, jh=jh, dc=dc, tt=tt, par=par: e.matmul(
                        self.ps_big[:, 1024 * par + 512 * jh:1024 * par + 512 * jh + 512],
                        O_fm[dc][:, 128 * tt:128 * tt + 128],
                        wo_sb[:, C * dc + 512 * jh:C * dc + 512 * jh + 512],
                        start=(dc == 0), stop=(dc == NDC - 1)),
                        reads=["wo"] + [f"O{dc}_{qb}" for qb in range(4)],
                        writes=[f"pb{2 * par + jh}"], tag=f"op{tt}_{jh}")

        def p4a_chain(tt):
            par = tt % 2
            op_ap, op_k = self.pb(1024 * par, 1024 * par + 1024)
            a1 = t["L1_a"][:, tt:tt + 1]
            b1 = t["L1_b"][:, tt:tt + 1]
            p.op("dve", lambda e, par=par, a1=a1, b1=b1: e.tensor_scalar(
                tb1[par][:], xo[par][:], a1, b1, ALU.mult, ALU.add),
                reads=[f"xo{par}", f"L1_a{tt}", f"L1_b{tt}"], writes=[f"tb1{par}"],
                tag=f"ln1r{tt}")
            p.op("dve", lambda e, par=par: e.tensor_add(tb1[par][:], tb1[par][:], b_bcast[:]),
                 reads=[f"tb1{par}", "b_bcast"], writes=[f"tb1{par}"])
            p.op("dve", lambda e, op_ap=op_ap, par=par: e.tensor_add(aob[par][:], tb1[par][:], op_ap),
                 reads=[f"tb1{par}"] + list(op_k), writes=[f"aob{par}"], tag=f"ao{tt}")
            a2, b2, abk2 = self.ln_stats(aob[par][:], [f"aob{par}"], "L2", tt, scr[:], "scr")
            self.ln_apply(zb[par][:], [f"zb{par}"], aob[par][:], [f"aob{par}"], a2, b2, abk2,
                          tag=f"ln2a{tt}")
            p.op("dve", lambda e, par=par: e.tensor_add(x1b[par][:], xo[par][:], zb[par][:]),
                 reads=[f"xo{par}", f"zb{par}"], writes=[f"x1b{par}"], tag=f"x1{tt}")
            p.op("qsp", lambda e, tt=tt, par=par: e.dma_start(
                out=self.x1sp[128 * tt:128 * tt + 128, :], in_=x1b[par][:]),
                reads=[f"x1b{par}"], writes=[f"x1sp{tt}"], tag=f"stx1{tt}")
            a3, b3, abk3 = self.ln_stats(x1b[par][:], [f"x1b{par}"], "L3", tt,
                                         scr[:], "scr")
            self.ln_apply(ln3c[par][:], [f"ln3c{par}"], x1b[par][:], [f"x1b{par}"], a3, b3, abk3,
                          tag=f"ln3a{tt}")

        def p4a_gates(tt):
            par = tt % 2
            g_ap, g_k = self.pa(1536, 1536 + NE)
            for dc in range(NDC):
                tpar = dc % 2
                tp_ap, tp_k = self.pa(512 * tpar, 512 * tpar + 128)
                p.op("pe", lambda e, dc=dc, par=par, tp_ap=tp_ap: e.transpose(
                    tp_ap, ln3c[par][:, 128 * dc:128 * dc + 128], ident[:]),
                    reads=[f"ln3c{par}", "ident"], writes=tp_k, tag=f"tr3_{tt}_{dc}")
                p.op("act", lambda e, tp_ap=tp_ap, tpar=tpar: e.activation(
                    tmpf[tpar][:], tp_ap, AF.Copy),
                    reads=tp_k, writes=[f"tmpf{tpar}"], tag=f"tr3e_{tt}_{dc}")
                p.op("act", lambda e, dc=dc, tt=tt, tp_ap=tp_ap: e.activation(
                    fm_buf[dc][:, 128 * tt:128 * tt + 128], tp_ap, AF.Copy),
                    reads=tp_k, writes=[f"fm{dc}_{tt}"], tag=f"cast3_{tt}_{dc}")
                p.op("pe", lambda e, dc=dc, g_ap=g_ap, tpar=tpar: e.matmul(
                    g_ap, tmpf[tpar][:], gw_sb[:, NE * dc:NE * dc + NE],
                    start=(dc == 0), stop=(dc == NDC - 1)),
                    reads=[f"tmpf{tpar}", "gw"], writes=g_k, tag=f"gate{tt}_{dc}")

        def p4a_lgev(tt):
            g_ap, g_k = self.pa(1536, 1536 + NE)
            lg = logits_all[:, NE * tt:NE * tt + NE]
            p.op("act", lambda e, g_ap=g_ap, lg=lg: e.activation(lg, g_ap, AF.Copy),
                 reads=g_k, writes=[f"lg{tt}"], tag=f"lgev{tt}")
            p.op("dve", lambda e, lg=lg, tt=tt: e.reduce_max(
                rmax_all[:, tt:tt + 1], lg, axis=AX.X),
                reads=[f"lg{tt}"], writes=[f"rmax{tt}"])

        p4a_mms(0)
        p4a_chain(0)
        for tt in range(NT_OWN):
            if tt + 1 < NT_OWN:
                p4a_mms(tt + 1)
            p4a_gates(tt)
            if tt + 1 < NT_OWN:
                p4a_chain(tt + 1)
            p4a_lgev(tt)

        p.op("act", lambda e: e.activation(scores[:], rmax_all[:], AF.Sigmoid),
             reads=[f"rmax{tt}" for tt in range(NT_OWN)], writes=["scores"],
             tag="sigmoid")
        p.op("dve", lambda e: e.tensor_mul(alpha[:], t["L3_a"][:], scores[:]),
             reads=["scores"] + [f"L3_a{tt}" for tt in range(NT_OWN)], writes=["alpha"])
        p.op("dve", lambda e: e.tensor_mul(beta[:], t["L3_b"][:], scores[:]),
             reads=["scores"] + [f"L3_b{tt}" for tt in range(NT_OWN)], writes=["beta"])

        def emit_masks():
            for tt in range(NT_OWN):
                lg = logits_all[:, NE * tt:NE * tt + NE]
                p.op("dve", lambda e, lg=lg, tt=tt: e.tensor_scalar(
                        eqb[:], lg, rmax_all[:, tt:tt + 1], None, ALU.is_equal),
                        reads=[f"lg{tt}", f"rmax{tt}"], writes=["eqb"], tag=f"eq{tt}")
                p.op("dve", lambda e: e.tensor_tensor_scan(
                        prefb[:], eqb[:], ones8[:], 0.0, ALU.add, ALU.mult),
                        reads=["eqb", "ones8"], writes=["prefb"], tag=f"scan{tt}")
                p.op("dve", lambda e: e.tensor_single_scalar(
                        prefb[:], prefb[:], 1.0, ALU.is_equal),
                        reads=["prefb"], writes=["prefb"])
                p.op("dve", lambda e, tt=tt: e.tensor_mul(
                        mask_all[:, NE * tt:NE * tt + NE], prefb[:], eqb[:]),
                        reads=["prefb", "eqb"], writes=[f"mask{tt}"], tag=f"mask{tt}")


        Ltri = self.at("Ltri", [128, 128], BF16)
        mask_em = self.at("mask_em", [128, NE * NT_OWN], F32)
        c_all = self.at("c_all", [128, NE * NT_OWN], F32)
        ones16 = self.at("ones16", [128, NT_OWN], F32)
        T8b = self.at("T8b", [128, NE], BF16)
        Pexcl = self.at("Pexcl", [128, NE], F32)
        slotg = self.at("slotg", [128, NT_OWN], F32)
        slotu = self.at("slotu", [128, NT_OWN], U32)
        selm = self.at("selm", [128, NT_OWN], F32)
        selmu = self.at("selmu", [128, NT_OWN], U8)
        t1b = self.at("t1b", [128, NT_OWN], F32)
        tok_iota = self.at("tok_iota", [128, NT_OWN], U32)
        pref_sb = self.at("pref_sb", [128, 32], U32)
        idx_sb = self.at("idx_sb", [128, 32], U32)

        def emit_routing_head():
            p.op("pool", lambda e: e.memset(Ltri[:], 0.0), writes=["Ltri"])
            p.op("pool", lambda e: e.affine_select(out=Ltri[:], in_=Ltri[:],
                                                   compare_op=ALU.is_ge, fill=1.0,
                                                   base=0, pattern=[[-1, 128]],
                                                   channel_multiplier=1),
                 reads=["Ltri"], writes=["Ltri"])
            p.op("pool", lambda e: e.memset(ones16[:], 1.0), writes=["ones16"])
            p.op("pool", lambda e: e.memset(slotg[:], 1e9), writes=["slotg"])
            p.op("pool", lambda e: e.iota(tok_iota[:], pattern=[[128, NT_OWN]],
                                          base=0, channel_multiplier=1),
                 writes=["toki"])
            p.op("pool", lambda e: e.iota(
                pref_sb[:], pattern=[[128, 32]], base=TOK, channel_multiplier=1),
                writes=["prefsb"])
            mview = mask_all[:].rearrange("p (t e2) -> p t e2", t=NT_OWN)
            for ei in range(NE):
                p.op("dve", lambda e, ei=ei: e.tensor_copy(
                    mask_em[:, NT_OWN * ei:NT_OWN * ei + NT_OWN], mview[:, :, ei]),
                    reads=[f"mask{tt}" for tt in range(NT_OWN)], writes=["mem"],
                    tag=f"mem{ei}")
                p.op("dve", lambda e, ei=ei: e.tensor_tensor_scan(
                    c_all[:, NT_OWN * ei:NT_OWN * ei + NT_OWN],
                    mask_em[:, NT_OWN * ei:NT_OWN * ei + NT_OWN],
                    ones16[:], 0.0, ALU.add, ALU.mult),
                    reads=["mem", "ones16"], writes=["call"], tag=f"cscan{ei}")
            cview = c_all[:].rearrange("p (e2 t) -> p e2 t", e2=NE)
            p.op("dve", lambda e: e.tensor_copy(T8b[:], cview[:, :, NT_OWN - 1]),
                 reads=["call"], writes=["T8b"])
        def emit_routing_tail():
            px_ap, px_k = self.pa(1536, 1536 + NE)
            p.op("pe", lambda e: e.matmul(px_ap, Ltri[:], T8b[:], start=True, stop=True),
                 reads=["Ltri", "T8b"], writes=px_k, tag="prefmm")
            p.op("dve", lambda e: e.tensor_copy(Pexcl[:], px_ap), reads=px_k,
                 writes=["Pexcl"])
            for ei in range(NE):
                ce = c_all[:, NT_OWN * ei:NT_OWN * ei + NT_OWN]
                me = mask_em[:, NT_OWN * ei:NT_OWN * ei + NT_OWN]
                p.op("dve", lambda e, ce=ce, ei=ei: e.tensor_scalar(
                    t1b[:], ce, Pexcl[:, ei:ei + 1], float(CAP * ei - 1), ALU.add, ALU.add),
                    reads=["call", "Pexcl"], writes=["t1b"], tag=f"slot{ei}")
                p.op("dve", lambda e, ei=ei: e.tensor_single_scalar(
                    selm[:], t1b[:], float(CAP * (ei + 1)), ALU.is_lt),
                    reads=["t1b"], writes=["selm"])
                p.op("dve", lambda e, me=me: e.tensor_mul(selm[:], selm[:], me),
                     reads=["selm", "mem"], writes=["selm"])
                p.op("dve", lambda e: e.tensor_copy(selmu[:], selm[:]),
                     reads=["selm"], writes=["selmu"])
                p.op("dve", lambda e: e.select(slotg[:], selmu[:], t1b[:], slotg[:],
                                               add_drain=True),
                     reads=["selmu", "t1b", "slotg"], writes=["slotg"], tag=f"ssel{ei}")
            p.op("dve", lambda e: e.tensor_copy(slotu[:], slotg[:]),
                 reads=["slotg"], writes=["slotu"])

        def emit_prefill():
            for k in range(NSLOT):
                p.op("qpool", lambda e, k=k: e.dma_start(
                    out=self.idx_dram[128 * k:128 * k + 128, :], in_=pref_sb[:, k:k + 1]),
                    reads=["prefsb"], writes=[f"idxd{k}"], tag=f"pfill{k}")
        def emit_index_build():
            pf_keys = [f"idxd{k}" for k in range(NSLOT)]
            for tt in range(NT_OWN):
                p.op("qpool", lambda e, tt=tt: e.indirect_dma_start(
                    out=self.idx_dram[:], out_offset=IndirectOffsetOnAxis(slotu[:, tt:tt + 1], 0),
                    in_=tok_iota[:, tt:tt + 1], in_offset=None,
                    bounds_check=self.rbnd["s"], oob_is_err=False),
                    reads=["slotu", "toki"] + pf_keys, writes=[f"isc{tt}"], tag=f"iscat{tt}")
            is_keys = [f"isc{tt}" for tt in range(NT_OWN)]
            for k in range(NSLOT):
                p.op("qpool", lambda e, k=k: e.dma_start(
                    out=idx_sb[:, k:k + 1], in_=self.idx_dram[128 * k:128 * k + 128, :]),
                    reads=[f"idxd{k}"] + is_keys, writes=[f"idx{k}"], tag=f"ldidx{k}")

        ffn_acc = self.at("ffn_acc", [128, NT_OWN * C], F32)
        pe_half = self.at("pe_half", [128, NFC * 1024], BF16)
        wgu_b = [self.at(f"wgu_b{i}", [128, NDC * 256], BF16) for i in range(2)]
        wd_b = [self.at(f"wd_b{i}", [128, NFC * 256], BF16) for i in range(2)]
        silb = [self.at(f"silb{i}", [128, 1024], BF16) for i in range(2)]
        guards = {
            "wd_b0": [f"O{dc}_{qb}" for dc in (0, 1) for qb in range(4)],
            "wd_b1": [f"O{dc}_{qb}" for dc in (2, 3) for qb in range(4)],
            "wgu_b0": [f"O{dc}_{qb}" for dc in (4, 5) for qb in range(4)],
            "wgu_b1": [f"O{dc}_{qb}" for dc in (4, 5) for qb in range(4)],
            "sil0": [f"O6_{qb}" for qb in range(4)],
            "sil1": [f"O6_{qb}" for qb in range(4)],
        }

        xsb = [(ln3b, "ln3b"), (scr, "scr")]

        def p4b_tile(tt):
            par = tt % 2
            p.op("qpool", lambda e, tt=tt, par=par: e.dma_start(
                out=x1b[par][:], in_=self.x1sp[128 * tt:128 * tt + 128, :]),
                reads=[f"x1sp{tt}"], writes=[f"x1b{par}"], tag=f"ldx1b{tt}")
            xb_, xkey = xsb[par]
            p.op("dve", lambda e, par=par, tt=tt, xb_=xb_: e.tensor_scalar(
                xb_[:], x1b[par][:], alpha[:, tt:tt + 1], beta[:, tt:tt + 1],
                ALU.mult, ALU.add),
                reads=[f"x1b{par}", "alpha", "beta"], writes=[xkey], tag=f"xs{tt}")
            p.op("qpool", lambda e, tt=tt, xb_=xb_: e.dma_start(
                out=self.xssp[128 * tt:128 * tt + 128, :], in_=xb_[:]),
                reads=[xkey], writes=["xssp"], tag=f"stxs{tt}")

        def ffn_expert(e_idx, mid_hook=None):
            nm = "sh" if e_idx < 0 else f"e{e_idx}"
            gusrc, guk_ = self.wsgu_bf, "wbfsgu"
            dsrc, dk_ = self.wsd_bf, "wbfsd"
            for hh in range(2):
                if hh == 1 and mid_hook is not None:
                    mid_hook()
                for f in range(NFC):
                    wpar = f % 2
                    gk = guards.pop(f"wgu_b{wpar}", [])
                    p.op("qsp", lambda e, f=f, wpar=wpar: e.dma_start(
                        out=wgu_b[wpar][:].rearrange("p (dc f2) -> p dc f2", dc=NDC),
                        in_=gusrc[:, f, :].rearrange("(dc p) f2 -> p dc f2", p=128)),
                        reads=[guk_], writes=[f"wgu{wpar}"] + gk,
                        tag=f"ldgu{nm}_{hh}_{f}")
                    hpar = f % 2
                    zg_ap, zg_k = (self.pb if hpar == 0 else self.pa)(0, 1024)
                    zu_ap, zu_k = (self.pb if hpar == 0 else self.pa)(1024, 2048)
                    zap = self.ps_big if hpar == 0 else self.ps_av
                    for sub in range(2):
                        tok0 = 1024 * hh + 512 * sub
                        for half, zk, cof in ((0, zg_k, 0), (1, zu_k, 1024)):
                            for dc in range(NDC):
                                p.op("pe", lambda e, half=half, dc=dc, tok0=tok0, zap=zap, cof=cof, sub=sub, wpar=wpar: e.matmul(
                                    zap[:, cof + 512 * sub:cof + 512 * sub + 512],
                                    wgu_b[wpar][:, 256 * dc + 128 * half:256 * dc + 128 * half + 128],
                                    fm_buf[dc][:, tok0:tok0 + 512],
                                    start=(dc == 0), stop=(dc == NDC - 1)),
                                    reads=[f"wgu{wpar}"] + [f"fm{dc}_{i}" for i in range(tok0 // 128, tok0 // 128 + 4)],
                                    writes=[zk[sub]], tag=f"z{half}mm{nm}_{hh}_{f}_{sub}")
                    gk = guards.pop(f"sil{hpar}", [])
                    p.op("act", lambda e, zg_ap=zg_ap, hpar=hpar: e.activation(
                        silb[hpar][:], zg_ap, AF.Silu),
                        reads=zg_k, writes=[f"sil{hpar}"] + gk, tag=f"sil{nm}_{hh}_{f}")
                    p.op("dve", lambda e, zu_ap=zu_ap, hpar=hpar, f=f: e.tensor_tensor(
                        pe_half[:, 1024 * f:1024 * f + 1024],
                        silb[hpar][:], zu_ap, ALU.mult),
                        reads=[f"sil{hpar}"] + list(zu_k), writes=[f"pe{f}"],
                        tag=f"p{nm}_{hh}_{f}")
                    if e_idx < 0 and hh == 0:
                        p4b_tile(f)
                for jo in range(4):
                    jpar = jo % 2
                    gk = guards.pop(f"wd_b{jpar}", [])
                    p.op("qsp", lambda e, dsrc=dsrc, jo=jo, jpar=jpar: e.dma_start(
                        out=wd_b[jpar][:].rearrange("p (fc o) -> p fc o", fc=NFC),
                        in_=dsrc[:, 256 * jo:256 * jo + 256]
                            .rearrange("(fc p) o -> p fc o", p=128)),
                        reads=[dk_], writes=[f"wd{jpar}"] + gk, tag=f"ldd{nm}_{hh}_{jo}")
                    for t8 in range(8):
                        tt = 8 * hh + t8
                        dpar = t8 % 2
                        d_ap, d_k = self.pb(512 * dpar, 512 * dpar + 256)
                        for fc in range(NFC):
                            p.op("pe", lambda e, d_ap=d_ap, fc=fc, t8=t8, jpar=jpar: e.matmul(
                                d_ap, pe_half[:, 1024 * fc + 128 * t8: 1024 * fc + 128 * t8 + 128],
                                wd_b[jpar][:, 256 * fc:256 * fc + 256],
                                start=(fc == 0), stop=(fc == NFC - 1)),
                                reads=[f"wd{jpar}", f"pe{fc}"], writes=d_k,
                                tag=f"dmm{nm}_{hh}_{jo}_{t8}")
                        acc_ap = ffn_acc[:, C * tt + 256 * jo: C * tt + 256 * jo + 256]
                        p.op("dve", lambda e, d_ap=d_ap, acc_ap=acc_ap: e.tensor_copy(
                            acc_ap, d_ap),
                            reads=d_k, writes=[f"acc{tt}_{jo}"],
                            tag=f"dev{nm}_{hh}_{jo}_{t8}")

        emit_masks()
        emit_routing_head()
        emit_prefill()
        ffn_expert(-1, mid_hook=lambda: (emit_routing_tail(),
                                         emit_index_build()))

        xef = [self.at(f"xef{dc}", [128, CAP], BF16) for dc in range(NDC)]
        ident_bf = self.at("ident_bf", [128, 128], BF16)
        gbuf = [self.at(f"gbuf{i}", [128, C], BF16) for i in range(3)]
        p.op("pool", lambda e: e.tensor_copy(ident_bf[:], ident[:]),
             reads=["ident"], writes=["identbf", "srow", "trow", "rsebf",
                                      "rsebf1", "rset0", "rset1"])
        xbufs = [(x1b[0], "x1b0"), (x1b[1], "x1b1"), (ln3b, "ln3b"), (scr, "scr")]
        self._gi = 0

        def sp_gather(e_idx):
            nm = f"se{e_idx}"
            for k in range(CAPK):
                gi = self._gi % 3
                self._gi += 1
                buf, bkey = gbuf[gi], f"gb{gi}"
                icol = idx_sb[:, CAPK * e_idx + k: CAPK * e_idx + k + 1]
                p.op("qpool", lambda e, buf=buf, icol=icol: e.indirect_dma_start(
                    out=buf[:], out_offset=None, in_=self.xssp[:],
                    in_offset=IndirectOffsetOnAxis(icol, 0),
                    bounds_check=self.rbnd["g"], oob_is_err=False),
                    reads=[f"idx{CAPK * e_idx + k}", "xssp", bkey], writes=[bkey],
                    tag=f"gx{nm}_{k}")
                for dc in range(NDC):
                    tpar = dc % 2
                    tp_f, tp_k = self.pa(512 * tpar, 512 * tpar + 64)
                    tp_bf = tp_f.bitcast(BF16)
                    p.op("pe", lambda e, buf=buf, dc=dc, tp_bf=tp_bf: e.transpose(
                        tp_bf, buf[:, 128 * dc:128 * dc + 128], ident_bf[:]),
                        reads=[bkey, "identbf"], writes=tp_k, tag=f"tx{nm}_{k}_{dc}")
                    p.op("dve", lambda e, dc=dc, k=k, tp_bf=tp_bf: e.tensor_copy(
                        xef[dc][:, 128 * k:128 * k + 128], tp_bf),
                        reads=tp_k, writes=[f"xef{dc}"], tag=f"txe{nm}_{k}_{dc}")

        gub = [wgu_b[0][:], wgu_b[1][:],
               wd_b[0][:, 0:NDC * 256], wd_b[1][:, 0:NDC * 256]]

        def sp_w_dma(e_idx, f):
            nm = f"se{e_idx}"
            w4 = f % 4
            keys = [f"wgu{w4}"] + ([f"wd{w4 - 2}"] if w4 >= 2 else [])
            p.op("qsp", lambda e, e_idx=e_idx, f=f, w4=w4: e.dma_start(
                out=gub[w4].rearrange("p (dc f2) -> p dc f2", dc=NDC),
                in_=self.wrgu_bf[e_idx][:, f, :].rearrange("(dc p) f2 -> p dc f2", p=128)),
                reads=[f"wbfgu{e_idx}"], writes=keys, tag=f"ldgu{nm}_{f}")

        def sp_gu(e_idx, skip_w=0):
            nm = f"se{e_idx}"
            for f in range(NFC):
                wpar = f % 2
                w4 = f % 4
                if f >= skip_w:
                    sp_w_dma(e_idx, f)
                zg_ap, zg_k = self.pb(1024 * wpar, 1024 * wpar + CAP)
                zu_ap, zu_k = self.pb(1024 * wpar + 512, 1024 * wpar + 512 + CAP)
                for half, zap, zk in ((0, zg_ap, zg_k), (1, zu_ap, zu_k)):
                    for dc in range(NDC):
                        p.op("pe", lambda e, half=half, dc=dc, zap=zap, w4=w4: e.matmul(
                            zap, gub[w4][:, 256 * dc + 128 * half:256 * dc + 128 * half + 128],
                            xef[dc][:],
                            start=(dc == 0), stop=(dc == NDC - 1)),
                            reads=[f"wgu{w4}", f"xef{dc}"],
                            writes=zk, tag=f"z{half}mm{nm}_{f}")
                p.op("act", lambda e, zg_ap=zg_ap, wpar=wpar: e.activation(
                    silb[wpar][:, 0:CAP], zg_ap, AF.Silu),
                    reads=zg_k, writes=[f"sil{wpar}"], tag=f"sil{nm}_{f}")
                p.op("dve", lambda e, zu_ap=zu_ap, wpar=wpar, f=f: e.tensor_tensor(
                    pe_half[:, 1024 * f:1024 * f + CAP],
                    silb[wpar][:, 0:CAP], zu_ap, ALU.mult),
                    reads=[f"sil{wpar}"] + list(zu_k), writes=[f"pe{f}"],
                    tag=f"p{nm}_{f}")

        def sp_down(e_idx):
            nm = f"se{e_idx}"
            dsrc = self.wrd_bf[e_idx]
            for jo in range(4):
                jpar = jo % 2
                p.op("qsp", lambda e, dsrc=dsrc, jo=jo, jpar=jpar: e.dma_start(
                    out=wd_b[jpar][:].rearrange("p (fc o) -> p fc o", fc=NFC),
                    in_=dsrc[:, 256 * jo:256 * jo + 256]
                        .rearrange("(fc p) o -> p fc o", p=128)),
                    reads=[f"wbfrd{e_idx}"], writes=[f"wd{jpar}", f"wgu{jpar + 2}"],
                    tag=f"ldd{nm}_{jo}")
                for t8 in range(CAPK):
                    buf, bkey = xbufs[t8]
                    d_ap, d_k = self.pa(1024 + 512 * (t8 % 2),
                                        1024 + 512 * (t8 % 2) + 256)
                    for fc in range(NFC):
                        p.op("pe", lambda e, d_ap=d_ap, fc=fc, t8=t8, jpar=jpar: e.matmul(
                            d_ap, pe_half[:, 1024 * fc + 128 * t8: 1024 * fc + 128 * t8 + 128],
                            wd_b[jpar][:, 256 * fc:256 * fc + 256],
                            start=(fc == 0), stop=(fc == NFC - 1)),
                            reads=[f"wd{jpar}", f"pe{fc}"], writes=d_k,
                            tag=f"dmm{nm}_{jo}_{t8}")
                    p.op("dve", lambda e, d_ap=d_ap, buf=buf, jo=jo: e.tensor_copy(
                        buf[:, 256 * jo:256 * jo + 256], d_ap),
                        reads=d_k + (bkey,), writes=[bkey], tag=f"dev{nm}_{jo}_{t8}")

        rz_keys = [f"rz{tt}" for tt in range(NT_OWN)]

        def sp_scatter(e_idx):
            nm = f"se{e_idx}"
            for t8 in range(CAPK):
                buf, bkey = xbufs[t8]
                kk = CAPK * e_idx + t8
                icol = idx_sb[:, kk:kk + 1]
                p.op("qpool", lambda e, buf=buf, icol=icol: e.indirect_dma_start(
                    out=self.rout_sp[:], out_offset=IndirectOffsetOnAxis(icol, 0),
                    in_=buf[:], in_offset=None),
                    reads=[f"idx{kk}", bkey] + rz_keys, writes=[f"rs{kk}"],
                    tag=f"rs{nm}_{t8}")

        sp_gather(0)
        for e_idx in range(NE):
            sp_gu(e_idx, skip_w=2 if e_idx > 0 else 0)
            if e_idx + 1 < NE:
                sp_gather(e_idx + 1)
                sp_w_dma(e_idx + 1, 0)
                sp_w_dma(e_idx + 1, 1)
            sp_down(e_idx)
            sp_scatter(e_idx)

        for t0 in range(0, NT_OWN, 4):
            akeys = [f"acc{tt}_{jo}" for tt in range(t0, t0 + 4) for jo in range(4)]
            p.op("qpool", lambda e, t0=t0: e.dma_start(
                out=ffn_acc[:, C * t0:C * t0 + 4 * C]
                    .rearrange("p (a c) -> p a c", a=4),
                in_=self.rout_sp[128 * t0:128 * t0 + 512, :]
                    .rearrange("(a p) c -> p a c", p=128), accum_op=ALU.add),
                reads=[f"rs{kk}" for kk in range(NSLOT)] + akeys,
                writes=akeys, tag=f"raccum{t0}")
        p6x = [self.nc.alloc_sbuf_tensor_at(f"p6b{i}", [128, C], F32,
                                            offset=_L["pe_half"] + 4096 * i)
               for i in range(8)]
        p6buf = [(x1b[0], "x1b0"), (x1b[1], "x1b1")] + \
                [(p6x[i], f"p6b{i}") for i in range(8)]
        p6guard = {i + 2: [f"pe{2 * i}", f"pe{2 * i + 1}"] for i in range(8)}

        def p6_s1(tt):
            xb_, xk_ = p6buf[tt % 10]
            gk = p6guard.pop(tt, [])
            p.op("qsp", lambda e, tt=tt, xb_=xb_: e.dma_start(
                out=xb_[:], in_=self.x1sp[128 * tt:128 * tt + 128, :]),
                reads=[f"x1sp{tt}"], writes=[xk_] + gk, tag=f"ldx1c{tt}")
            f_ap = ffn_acc[:, C * tt:C * tt + C]
            fkeys = [f"acc{tt}_{jo}" for jo in range(4)]
            s_ = t["L4_s"][:, tt:tt + 1]
            ss = t["L4_ss"][:, tt:tt + 1]
            p.op("dve", lambda e, f_ap=f_ap, s_=s_: e.reduce_sum(s_, f_ap, axis=AX.X),
                 reads=fkeys, writes=[f"L4_s{tt}"], tag=f"L4{tt}_rs")
            p.op("act", lambda e, f_ap=f_ap, ss=ss: e.activation(
                scr[:], f_ap, AF.Square, accum_out=ss),
                reads=fkeys + ["scr"], writes=[f"L4_ss{tt}", "scr"], tag=f"L4{tt}_sq")

        def p6_s2(tt):
            s_ = t["L4_s"][:, tt:tt + 1]
            ss = t["L4_ss"][:, tt:tt + 1]
            m_ = t["L4_m"][:, tt:tt + 1]
            mq = t["L4_mq"][:, tt:tt + 1]
            v_ = t["L4_v"][:, tt:tt + 1]
            p.op("dve", lambda e, s_=s_, m_=m_: e.tensor_single_scalar(m_, s_, 1.0 / C, ALU.mult),
                 reads=[f"L4_s{tt}"], writes=[f"L4_m{tt}"])
            p.op("dve", lambda e, m_=m_, mq=mq: e.tensor_mul(mq, m_, m_),
                 reads=[f"L4_m{tt}"], writes=[f"L4_mq{tt}"])
            p.op("dve", lambda e, ss=ss, mq=mq, v_=v_: e.tensor_scalar(
                v_, ss, 1.0 / C, mq, ALU.mult, ALU.subtract),
                reads=[f"L4_ss{tt}", f"L4_mq{tt}"], writes=[f"L4_v{tt}"])

        def p6_s3(tt):
            v_ = t["L4_v"][:, tt:tt + 1]
            lv = t["L4_lv"][:, tt:tt + 1]
            a_ = t["L4_a"][:, tt:tt + 1]
            p.op("act", lambda e, v_=v_, lv=lv: e.activation(lv, v_, AF.Ln, bias=self.epsc[:]),
                 reads=[f"L4_v{tt}", "epsc"], writes=[f"L4_lv{tt}"])
            p.op("act", lambda e, lv=lv, a_=a_: e.activation(a_, lv, AF.Exp, scale=-0.5),
                 reads=[f"L4_lv{tt}"], writes=[f"L4_a{tt}"])

        def p6_s4(tt):
            xb_, xk_ = p6buf[tt % 10]
            m_ = t["L4_m"][:, tt:tt + 1]
            a4 = t["L4_a"][:, tt:tt + 1]
            b4 = t["L4_b"][:, tt:tt + 1]
            f_ap = ffn_acc[:, C * tt:C * tt + C]
            fkeys = [f"acc{tt}_{jo}" for jo in range(4)]
            p.op("dve", lambda e, m_=m_, a4=a4, b4=b4: e.tensor_scalar(
                b4, m_, a4, -1.0, ALU.mult, ALU.mult),
                reads=[f"L4_m{tt}", f"L4_a{tt}"], writes=[f"L4_b{tt}"])
            p.op("act", lambda e, xb_=xb_, b4=b4: e.activation(
                xb_[:], xb_[:], AF.Identity, scale=1.0, bias=b4),
                reads=[xk_, f"L4_b{tt}"], writes=[xk_], tag=f"yb{tt}")
            p.op("dve", lambda e, xb_=xb_, a4=a4, f_ap=f_ap: e.scalar_tensor_tensor(
                xb_[:], f_ap, a4, xb_[:], ALU.mult, ALU.add),
                reads=fkeys + [f"L4_a{tt}", xk_], writes=[xk_], tag=f"ya{tt}")
            p.op("qsp", lambda e, tt=tt, xb_=xb_: e.dma_start(
                out=self.y[128 * tt:128 * tt + 128, :], in_=xb_[:]),
                reads=[xk_], tag=f"sty{tt}")

        for i in range(NT_OWN + 3):
            if i < NT_OWN:
                p6_s1(i)
            if 0 <= i - 1 < NT_OWN:
                p6_s2(i - 1)
            if 0 <= i - 2 < NT_OWN:
                p6_s3(i - 2)
            if 0 <= i - 3 < NT_OWN:
                p6_s4(i - 3)

        self.p.finalize()
        return self.p.final_dma_waits("sync", "qsp")


DEBUG_TAPS = []


def build_program():
    nc = bass.Bass("TRN2", target_bir_lowering=False)
    b = Builder(nc)
    final_waits = b.build()
    for name, shape, dt_, fn in DEBUG_TAPS:
        dto = nc.dram_tensor(f"dbg_{name}", list(shape), dt_, kind="ExternalOutput")
        ap, keys = fn(b)
        b.p.op("qsp", lambda e, dto=dto, ap=ap: e.dma_start(out=dto[:], in_=ap),
               reads=keys, tag=f"dbg_{name}")
    if DEBUG_TAPS:
        b.p.finalize()
        final_waits = b.p.final_dma_waits("sync", "qsp")
    b.p.emit(nc, final_waits=final_waits, verbose=True)
    return nc


def _prep_inputs(inputs):
    x = np.asarray(inputs["L0_attn_x"], np.float32)
    idx = np.arange(3 * C).reshape(NH, 3, HD)
    order = np.concatenate([idx[:, 0].ravel(), idx[:, 1].ravel(), idx[:, 2].ravel()])
    wqT = np.ascontiguousarray(np.asarray(inputs["L0_attn_W_qkv"], np.float32)[order].T)
    woT = np.ascontiguousarray(np.asarray(inputs["L0_attn_W_out"], np.float32).T)
    bo = np.asarray(inputs["L0_attn_b_out"], np.float32).reshape(1, C)
    gwT = np.ascontiguousarray(np.asarray(inputs["L0_ffn_gate_w"], np.float32).T)
    wsgT = np.ascontiguousarray(np.asarray(inputs["L0_ffn_shared_gate"], np.float32).T)
    wsuT = np.ascontiguousarray(np.asarray(inputs["L0_ffn_shared_up"], np.float32).T)
    wsdT = np.ascontiguousarray(np.asarray(inputs["L0_ffn_shared_down"], np.float32).T)
    wrgT = np.ascontiguousarray(np.asarray(inputs["L0_ffn_r_gate"], np.float32).transpose(0, 2, 1))
    wruT = np.ascontiguousarray(np.asarray(inputs["L0_ffn_r_up"], np.float32).transpose(0, 2, 1))
    wrdT = np.ascontiguousarray(np.asarray(inputs["L0_ffn_r_down"], np.float32).transpose(0, 2, 1))
    shared = dict(wqT=wqT, woT=woT, bo=bo, gwT=gwT, wsgT=wsgT, wsuT=wsuT,
                  wsdT=wsdT, wrgT=wrgT, wruT=wruT, wrdT=wrdT)
    in_maps = []
    for core in range(8):
        bb, h = core // 2, core % 2
        own = x[bb, 2048 * h:2048 * h + 2048]
        oth = x[bb, 2048 * (1 - h):2048 * (1 - h) + 2048]
        xc = np.ascontiguousarray(np.concatenate([own, oth], axis=0))
        m = dict(shared)
        m["x_in"] = xc
        in_maps.append(m)
    return in_maps


def kernel(**inputs):
    in_maps = _prep_inputs(inputs)
    nc = build_program()
    res = run_bass_kernel_spmd(nc, in_maps, core_ids=list(range(8)))
    out = np.empty((B, S, C), np.float32)
    for core in range(8):
        bb, h = core // 2, core % 2
        out[bb, 2048 * h:2048 * h + 2048] = res.results[core]["y"]
    return out
